# version 31
# baseline (speedup 1.0000x reference)
from contextlib import ExitStack

import numpy as np
import concourse.bass as bass
import concourse.mybir as mybir
from concourse.bass_utils import run_bass_kernel_spmd

F32 = mybir.dt.float32
BF16 = mybir.dt.bfloat16
I8 = mybir.dt.int8
AF = mybir.ActivationFunctionType
ALU = mybir.AluOpType
AX = mybir.AxisListType

B, L, D, DI, S, H = 2, 8192, 1024, 2048, 16, 4096
TOK = 2048
CH = 512
NCH = TOK // CH
EPS = 1e-5
DJ = D // 128
IJ = DI // 128
HJ = H // 128


class Sem:
    def __init__(self, h):
        self.h = h
        self.n = 0

    def inc(self, instr, k=1):
        instr.then_inc(self.h, k)
        self.n += k
        return self.n

    def wait(self, eng, n=None):
        eng.wait_ge(self.h, self.n if n is None else n)


class EC:

    def __init__(self, eng, sem):
        self._e = eng
        self._s = sem

    def __getattr__(self, name):
        f = getattr(self._e, name)

        def g(*a, **kw):
            self._s.wait(self._e)
            ins = f(*a, **kw)
            self._s.inc(ins)
            return ins

        return g




from contextlib import contextmanager


class Ctx:
    def __init__(self, nc, block, sems):
        self.nc = nc
        self.block = block
        self.sems = sems


@contextmanager
def _phase_ctx(nc, cx):
    if cx is None:
        with (
            nc.semaphore("s_d") as _sd,
            nc.semaphore("s_t") as _st,
            nc.semaphore("s_v") as _sv,
            nc.semaphore("s_s") as _ss,
            nc.semaphore("s_g") as _sg,
            nc.semaphore("s_o") as _so,
            nc.Block() as block,
        ):
            yield block, (Sem(_sd), Sem(_st), Sem(_sv), Sem(_ss), Sem(_sg), Sem(_so))
    else:
        yield cx.block, cx.sems


def emit_fence(block, sems):
    for engname in ("sync", "tensor", "vector", "scalar", "gpsimd"):
        def _f(e, sems=sems):
            for s in sems:
                s.wait(e)
        getattr(block, engname)(_f)



def build_l1(cx=None, io=None):
    if cx is None:
        nc = bass.Bass("TRN2", target_bir_lowering=False)
        io = {}
        io["x_f"] = nc.dram_tensor("x_f", [128, DJ, TOK + 3], F32, kind="ExternalInput").ap()
        io["w_in"] = nc.dram_tensor("w_in", [128, DJ, 32, 128], F32, kind="ExternalInput").ap()
        io["w_delta"] = nc.dram_tensor("w_delta", [128, IJ, IJ, 128], F32, kind="ExternalInput").ap()
        io["w_bc"] = nc.dram_tensor("w_bc", [128, IJ, 32], F32, kind="ExternalInput").ap()
        io["ln_g"] = nc.dram_tensor("ln_g", [128, DJ], F32, kind="ExternalInput").ap()
        io["ln_b"] = nc.dram_tensor("ln_b", [128, DJ], F32, kind="ExternalInput").ap()
        io["conv_w"] = nc.dram_tensor("conv_w", [128, IJ, 4], F32, kind="ExternalInput").ap()
        io["conv_b"] = nc.dram_tensor("conv_b", [128, IJ], F32, kind="ExternalInput").ap()
        io["b_delta"] = nc.dram_tensor("b_delta", [128, IJ], F32, kind="ExternalInput").ap()
        io["ones_col"] = nc.dram_tensor("ones_col", [128, 1], F32, kind="ExternalInput").ap()
        io["ones_row"] = nc.dram_tensor("ones_row", [1, 128], F32, kind="ExternalInput").ap()
        io["halo_mask"] = nc.dram_tensor("halo_mask", [128, 1], F32, kind="ExternalInput").ap()
        io["o_delta"] = nc.dram_tensor("o_delta", [128, IJ, TOK], F32, kind="ExternalOutput").ap()
        io["o_dx"] = nc.dram_tensor("o_dx", [128, IJ, TOK], F32, kind="ExternalOutput").ap()
        io["o_xb"] = nc.dram_tensor("o_xb", [128, IJ, TOK], F32, kind="ExternalOutput").ap()
        io["o_z"] = nc.dram_tensor("o_z", [128, IJ, TOK], F32, kind="ExternalOutput").ap()
        io["o_bc"] = nc.dram_tensor("o_bc", [32, TOK], F32, kind="ExternalOutput").ap()
        io["xbb_hbm"] = nc.dram_tensor("xbb_hbm", [128, IJ, TOK], BF16, kind="Internal").ap()
        io["odelta_slice"] = lambda m, ci: io["o_delta"][:, m, ci * CH:(ci + 1) * CH]
        io["odx_slice"] = lambda m, ci: io["o_dx"][:, m, ci * CH:(ci + 1) * CH]
    else:
        nc = cx.nc
    x_f = io["x_f"]; w_in = io["w_in"]; w_delta = io["w_delta"]; w_bc = io["w_bc"]
    ln_g = io["ln_g"]; ln_b = io["ln_b"]; conv_w = io["conv_w"]; conv_b = io["conv_b"]
    b_delta = io["b_delta"]; ones_col = io["ones_col"]; ones_row = io["ones_row"]
    halo_mask = io["halo_mask"]
    o_xb = io["o_xb"]; o_z = io["o_z"]
    o_bc = io["o_bc"]; xbb_hbm = io["xbb_hbm"]

    es = ExitStack()
    sbt = lambda name, shape, dt=F32: es.enter_context(nc.sbuf_tensor("p1" + name, shape, dt))
    psb = lambda name, shape: es.enter_context(nc.psum_tensor("p1" + name, shape, F32))

    wa = sbt("wa", [128, DJ, 32, 128], BF16)
    wstage = sbt("wstage", [128, 16, 128], F32)
    wbc = sbt("wbc", [128, IJ, 32], BF16)
    lg = sbt("lg", [128, DJ]); lb = sbt("lb", [128, DJ])
    cw = sbt("cw", [128, IJ, 4]); cb = sbt("cb", [128, IJ])
    bd = sbt("bd", [128, IJ])
    onec = sbt("onec", [128, 1]); oner = sbt("oner", [1, 128])
    hmask = sbt("hmask", [128, 1])
    kc = sbt("kc", [128, 4])
    rows = sbt("rows", [1, 8, CH])
    tail = sbt("tail", [128, IJ, 3])
    spa = sbt("spa", [128, 2, CH]); spb = sbt("spb", [128, 2, CH])
    bcst = sbt("bcst", [32, CH], BF16)
    xbf = sbt("xbf", [128, IJ, CH])
    xbb = sbt("xbb", [128, IJ, CH], BF16)
    xh = sbt("xh", [128, DJ, 3])

    ps_mu = psb("ps_mu", [128, CH]); ps_m2 = psb("ps_m2", [128, CH])
    ps_b0 = psb("ps_b0", [128, CH]); ps_b1 = psb("ps_b1", [128, CH])
    ps_mm0 = psb("ps_mm0", [128, CH]); ps_mm1 = psb("ps_mm1", [128, CH])
    ps_bc = psb("ps_bc", [32, CH])

    with _phase_ctx(nc, cx) as (block, sems):
        sd, st, sv, ss, sg, so = sems

        @block.vector
        def _(v):
            V = EC(v, sv)
            V.memset(kc[:, 0:1], EPS)
            V.memset(kc[:, 1:2], 1.0)
            V.memset(kc[:, 2:3], -1.0)
            V.memset(kc[:, 3:4], 0.5)

        @block.sync
        def _(sy):
            for t, src in [(lg, ln_g), (lb, ln_b), (cw, conv_w), (cb, conv_b),
                           (bd, b_delta), (onec, ones_col), (oner, ones_row),
                           (hmask, halo_mask)]:
                sd.inc(sy.dma_start(out=t[:], in_=src), 16)
            sd.inc(sy.dma_start(out=xh[:], in_=x_f[:, :, 0:3]), 16)

        for k in range(DJ):
            for hh in range(2):
                @block.sync
                def _(sy, k=k, hh=hh):
                    sv.wait(sy)
                    sd.inc(sy.dma_start(out=wstage[:],
                                        in_=w_in[:, k, hh * 16:(hh + 1) * 16, :]), 16)

                @block.vector
                def _(v, k=k, hh=hh):
                    V = EC(v, sv)
                    sd.wait(v)
                    V.tensor_copy(out=wa[:, k, hh * 16:(hh + 1) * 16, :], in_=wstage[:])

        @block.sync
        def _(sy):
            sv.wait(sy)
            sd.inc(sy.dma_start(out=wstage[:, 0:IJ, 0:32], in_=w_bc[:]), 16)

        @block.vector
        def _(v):
            V = EC(v, sv)
            sd.wait(v)
            V.tensor_copy(out=wbc[:], in_=wstage[:, 0:IJ, 0:32])

        def emit_ln12(xw, w, sqbuf, xn1, xn2b):
            for j in range(DJ):
                @block.scalar
                def _(s, j=j):
                    S_ = EC(s, ss)
                    if j == 0:
                        sd.wait(s)
                        so.wait(s)
                    st.wait(s)
                    S_.activation(out=sqbuf[:, 0:w], in_=xw[:, j, 0:w],
                                  func=AF.Square)

                @block.tensor
                def _(t, j=j):
                    ss.wait(t)
                    if j == 0:
                        sv.wait(t)
                    t.matmul(ps_mu[0:1, 0:w], onec[:], xw[:, j, 0:w],
                             start=(j == 0), stop=(j == DJ - 1))
                    st.inc(t.matmul(ps_m2[0:1, 0:w], onec[:], sqbuf[:, 0:w],
                                    start=(j == 0), stop=(j == DJ - 1)))

            @block.vector
            def _(v):
                V = EC(v, sv)
                st.wait(v)
                V.tensor_copy(out=rows[:, 0, 0:w], in_=ps_mu[0:1, 0:w])
                V.tensor_copy(out=rows[:, 1, 0:w], in_=ps_m2[0:1, 0:w])
                V.tensor_tensor(out=rows[:, 2, 0:w], in0=rows[:, 0, 0:w],
                                in1=rows[:, 0, 0:w], op=ALU.mult)
                V.tensor_tensor(out=rows[:, 2, 0:w], in0=rows[:, 1, 0:w],
                                in1=rows[:, 2, 0:w], op=ALU.subtract)

            @block.scalar
            def _(s):
                S_ = EC(s, ss)
                sv.wait(s)
                S_.activation(out=rows[:, 3, 0:w], in_=rows[:, 2, 0:w],
                              func=AF.Sqrt, bias=kc[0:1, 0:1])

            @block.vector
            def _(v):
                V = EC(v, sv)
                ss.wait(v)
                V.reciprocal(out=rows[:, 4, 0:w], in_=rows[:, 3, 0:w])
                V.tensor_tensor(out=rows[:, 5, 0:w], in0=rows[:, 4, 0:w],
                                in1=rows[:, 4, 0:w], op=ALU.mult)
                V.tensor_tensor(out=rows[:, 5, 0:w], in0=rows[:, 2, 0:w],
                                in1=rows[:, 5, 0:w], op=ALU.mult)

            @block.scalar
            def _(s):
                S_ = EC(s, ss)
                sv.wait(s)
                S_.activation(out=rows[:, 6, 0:w], in_=rows[:, 5, 0:w],
                              func=AF.Sqrt, bias=kc[0:1, 0:1])

            @block.vector
            def _(v):
                V = EC(v, sv)
                ss.wait(v)
                V.reciprocal(out=rows[:, 6, 0:w], in_=rows[:, 6, 0:w])
                V.tensor_tensor(out=rows[:, 6, 0:w], in0=rows[:, 4, 0:w],
                                in1=rows[:, 6, 0:w], op=ALU.mult)
                V.tensor_tensor(out=rows[:, 7, 0:w], in0=rows[:, 0, 0:w],
                                in1=rows[:, 6, 0:w], op=ALU.mult)

            @block.tensor
            def _(t):
                sv.wait(t)
                t.matmul(ps_b1[:, 0:w], oner[:], rows[:, 6, 0:w], start=True, stop=True)
                st.inc(t.matmul(ps_b0[:, 0:w], oner[:], rows[:, 7, 0:w],
                                start=True, stop=True))

            @block.vector
            def _(v):
                V = EC(v, sv)
                st.wait(v)
                for j in range(DJ):
                    V.tensor_tensor(out=xn1[:, j, 0:w], in0=xw[:, j, 0:w],
                                    in1=ps_b1[:, 0:w], op=ALU.mult)
                    V.tensor_tensor(out=xn1[:, j, 0:w], in0=xn1[:, j, 0:w],
                                    in1=ps_b0[:, 0:w], op=ALU.subtract)
                    V.tensor_scalar(out=xn2b[:, j, 0:w], in0=xn1[:, j, 0:w],
                                    scalar1=lg[:, j:j + 1], scalar2=lb[:, j:j + 1],
                                    op0=ALU.mult, op1=ALU.add)

        with (
            nc.sbuf_tensor("xt", [128, DJ, CH], F32) as xt,
            nc.sbuf_tensor("xn1_t", [128, DJ, CH], F32) as xn1,
            nc.sbuf_tensor("xn2b_t", [128, DJ, CH], BF16) as xn2b,
            nc.sbuf_tensor("xbr", [128, 2, CH + 3], F32) as xbr,
            nc.sbuf_tensor("xcv", [128, 2, CH], F32) as xcv,
            nc.sbuf_tensor("zst", [128, 2, CH], F32) as zst,
        ):
            emit_ln12(xh, 3, spa[:, 0, :], xn1, xn2b)
            for m in range(IJ):
                ps = ps_mm0 if m % 2 == 0 else ps_mm1

                @block.tensor
                def _(t, m=m, ps=ps):
                    ss.wait(t)
                    sv.wait(t)
                    mm = None
                    for k in range(DJ):
                        mm = t.matmul(ps[:, 0:3], wa[:, k, m], xn2b[:, k, 0:3],
                                      start=(k == 0), stop=(k == DJ - 1))
                    st.inc(mm)

                @block.vector
                def _(v, m=m, ps=ps):
                    V = EC(v, sv)
                    st.wait(v)
                    V.tensor_copy(out=tail[:, m, :], in_=ps[:, 0:3])

            @block.vector
            def _(v):
                V = EC(v, sv)
                V.tensor_scalar(out=tail[:].rearrange("p a b -> p (a b)"),
                                in0=tail[:].rearrange("p a b -> p (a b)"),
                                scalar1=hmask[:], scalar2=None, op0=ALU.mult)

            for ci in range(NCH):
                c0 = ci * CH

                @block.sync
                def _(sy, c0=c0):
                    sv.wait(sy)
                    ss.wait(sy)
                    st.wait(sy)
                    so.wait(sy)
                    sd.inc(sy.dma_start(out=xt[:],
                                        in_=x_f[:, :, 3 + c0:3 + c0 + CH]), 16)

                emit_ln12(xt, CH, xbf[:, 0, :], xn1, xn2b)

                xn2b_ready = sv.n
                ps_freeA = [sv.n, sv.n]
                xcv_free = [ss.n, ss.n]
                for m in range(IJ):
                    ps = ps_mm0 if m % 2 == 0 else ps_mm1
                    pp = m % 2

                    @block.tensor
                    def _(t, m=m, ps=ps, mv=ps_freeA[m % 2], ms=xn2b_ready):
                        sv.wait(t, max(mv, ms))
                        mm = None
                        for k in range(DJ):
                            mm = t.matmul(ps[:, :], wa[:, k, m], xn2b[:, k, :],
                                          start=(k == 0), stop=(k == DJ - 1))
                        st.inc(mm)

                    @block.vector
                    def _(v, m=m, ps=ps, pp=pp, mx=xcv_free[m % 2]):
                        V = EC(v, sv)
                        st.wait(v)
                        ss.wait(v, mx)
                        V.tensor_copy(out=xbr[:, pp, 0:3], in_=tail[:, m, :])
                        V.tensor_copy(out=xbr[:, pp, 3:3 + CH], in_=ps[:, :])
                        V.tensor_copy(out=tail[:, m, :], in_=xbr[:, pp, CH:CH + 3])
                        V.tensor_scalar(out=xcv[:, pp, :], in0=xbr[:, pp, 0:CH],
                                        scalar1=cw[:, m, 0:1], scalar2=None,
                                        op0=ALU.mult)
                        for kk in range(1, 4):
                            V.scalar_tensor_tensor(
                                out=xcv[:, pp, :], in0=xbr[:, pp, kk:kk + CH],
                                scalar=cw[:, m, kk:kk + 1], in1=xcv[:, pp, :],
                                op0=ALU.mult, op1=ALU.add)

                    ps_freeA[pp] = sv.n

                    @block.scalar
                    def _(s, m=m, pp=pp):
                        S_ = EC(s, ss)
                        sv.wait(s)
                        S_.activation(out=xbf[:, m, :], in_=xcv[:, pp, :],
                                      func=AF.Silu, bias=cb[:, m:m + 1])

                    xcv_free[pp] = ss.n

                    @block.vector
                    def _(v, m=m):
                        V = EC(v, sv)
                        ss.wait(v)
                        V.tensor_copy(out=xbb[:, m, :], in_=xbf[:, m, :])

                ps_freeZ = [ss.n, ss.n]
                z_free = [so.n, so.n]
                xbr_done = sv.n
                for m in range(IJ, 32):
                    ps = ps_mm0 if m % 2 == 0 else ps_mm1
                    pp = m % 2

                    @block.tensor
                    def _(t, m=m, ps=ps, ms=ps_freeZ[m % 2], mv=xbr_done):
                        ss.wait(t, ms)
                        if m < IJ + 2:
                            sv.wait(t, mv)
                        mm = None
                        for k in range(DJ):
                            mm = t.matmul(ps[:, :], wa[:, k, m], xn2b[:, k, :],
                                          start=(k == 0), stop=(k == DJ - 1))
                        st.inc(mm)

                    @block.scalar
                    def _(s, ps=ps, pp=pp, mo=z_free[m % 2]):
                        S_ = EC(s, ss)
                        st.wait(s)
                        so.wait(s, mo)
                        S_.copy(out=zst[:, pp, :], in_=ps[:, :])

                    ps_freeZ[pp] = ss.n

                    @block.sync
                    def _(sy, m=m, ci=ci, pp=pp):
                        ss.wait(sy)
                        so.inc(sy.dma_start(out=o_z[:, m - IJ, ci * CH:(ci + 1) * CH],
                                            in_=zst[:, pp, :]), 16)

                    z_free[pp] = so.n

                @block.sync
                def _(sy, ci=ci):
                    ss.wait(sy)
                    sv.wait(sy)
                    so.inc(sy.dma_start(out=o_xb[:, :, ci * CH:(ci + 1) * CH],
                                        in_=xbf[:]), 16)
                    so.inc(sy.dma_start(out=xbb_hbm[:, :, ci * CH:(ci + 1) * CH],
                                        in_=xbb[:]), 16)

        wdv = wa[:].rearrange("p a b c -> p (a b c)").rearrange(
            "p (a b c) -> p a b c", a=IJ, b=IJ)

        with (
            nc.sbuf_tensor("dstg", [128, 2, CH], F32) as dstg,
            nc.sbuf_tensor("dxstg", [128, 2, CH], F32) as dxstg,
        ):
            cast_done = [sv.n, sv.n]
            for k in range(IJ):
                for hh in range(2):
                    i = k * 2 + hh
                    slot = i % 2

                    @block.sync
                    def _(sy, k=k, hh=hh, slot=slot, m=cast_done[slot]):
                        sv.wait(sy, m)
                        sd.inc(sy.dma_start(
                            out=wstage[:, slot * 8:slot * 8 + 8, :],
                            in_=w_delta[:, k, hh * 8:(hh + 1) * 8, :]), 16)

                    dm = sd.n

                    @block.vector
                    def _(v, k=k, hh=hh, slot=slot, i=i, dm=dm):
                        V = EC(v, sv)
                        sd.wait(v, dm)
                        if i == 0:
                            st.wait(v)
                        V.tensor_copy(out=wdv[:, k, hh * 8:(hh + 1) * 8, :],
                                      in_=wstage[:, slot * 8:slot * 8 + 8, :])

                    cast_done[slot] = sv.n

            for ci in range(NCH):
                @block.sync
                def _(sy, ci=ci):
                    sv.wait(sy)
                    so.wait(sy)
                    sd.inc(sy.dma_start(out=xbb[:],
                                        in_=xbb_hbm[:, :, ci * CH:(ci + 1) * CH]), 16)
                    sd.inc(sy.dma_start(out=xbf[:],
                                        in_=o_xb[:, :, ci * CH:(ci + 1) * CH]), 16)

                @block.tensor
                def _(t, ci=ci):
                    sd.wait(t)
                    sv.wait(t)
                    ss.wait(t)
                    mm = None
                    for k in range(IJ):
                        mm = t.matmul(ps_bc[:, :], wbc[:, k], xbb[:, k, :],
                                      start=(k == 0), stop=(k == IJ - 1))
                    st.inc(mm)

                @block.scalar
                def _(s):
                    S_ = EC(s, ss)
                    st.wait(s)
                    S_.copy(out=bcst[:], in_=ps_bc[:, :])

                @block.sync
                def _(sy, ci=ci):
                    ss.wait(sy)
                    so.inc(sy.dma_start(out=o_bc[:, ci * CH:(ci + 1) * CH],
                                        in_=bcst[:]), 16)

                in_ready = max(sv.n, 0)
                ps_freeB = [ss.n, ss.n]
                sp_free = [sv.n, sv.n]
                d_free = [so.n, so.n]
                for m in range(IJ):
                    ps = ps_mm0 if m % 2 == 0 else ps_mm1
                    pp = m % 2

                    @block.tensor
                    def _(t, m=m, ps=ps, ms=ps_freeB[m % 2], mv=in_ready):
                        ss.wait(t, ms)
                        sv.wait(t, mv)
                        mm = None
                        for k in range(IJ):
                            mm = t.matmul(ps[:, :], wdv[:, k, m], xbb[:, k, :],
                                          start=(k == 0), stop=(k == IJ - 1))
                        st.inc(mm)

                    @block.scalar
                    def _(s, m=m, ps=ps, mv=sp_free[m % 2]):
                        S_ = EC(s, ss)
                        st.wait(s)
                        sv.wait(s, mv)
                        pp2 = m % 2
                        S_.activation(out=spa[:, pp2, :], in_=ps[:, :], func=AF.Abs,
                                      bias=bd[:, m:m + 1])
                        S_.activation(out=spa[:, pp2, :], in_=spa[:, pp2, :],
                                      func=AF.Exp, scale=kc[:, 2:3])
                        S_.activation(out=spa[:, pp2, :], in_=spa[:, pp2, :],
                                      func=AF.Ln, bias=kc[:, 1:2])
                        S_.activation(out=spb[:, pp2, :], in_=ps[:, :], func=AF.Relu,
                                      bias=bd[:, m:m + 1])

                    ps_freeB[pp] = ss.n

                    @block.vector
                    def _(v, m=m, pp=pp, mo=d_free[m % 2]):
                        V = EC(v, sv)
                        ss.wait(v)
                        sd.wait(v)
                        so.wait(v, mo)
                        V.tensor_tensor(out=dstg[:, pp, :], in0=spa[:, pp, :],
                                        in1=spb[:, pp, :], op=ALU.add)
                        V.tensor_tensor(out=dxstg[:, pp, :], in0=dstg[:, pp, :],
                                        in1=xbf[:, m, :], op=ALU.mult)

                    sp_free[pp] = sv.n

                    @block.sync
                    def _(sy, m=m, ci=ci, pp=pp):
                        sv.wait(sy)
                        so.inc(sy.dma_start(out=io["odelta_slice"](m, ci),
                                            in_=dstg[:, pp, :]), 16)
                        so.inc(sy.dma_start(out=io["odx_slice"](m, ci),
                                            in_=dxstg[:, pp, :]), 16)

                    d_free[pp] = so.n

        @block.sync
        def _(sy):
            so.wait(sy)

    es.close()
    return nc


def host_l1_inputs(inputs):
    x = np.asarray(inputs["L0_attn_x"], np.float32)
    w_in = np.asarray(inputs["L0_attn_W_in"], np.float32)
    w_delta = np.asarray(inputs["L0_attn_W_delta"], np.float32)
    w_b = np.asarray(inputs["L0_attn_W_b"], np.float32)
    w_c = np.asarray(inputs["L0_attn_W_c"], np.float32)
    ln_g = np.asarray(inputs["L0_attn_ln_g"], np.float32)
    ln_b = np.asarray(inputs["L0_attn_ln_b"], np.float32)
    conv_w = np.asarray(inputs["L0_attn_conv_w"], np.float32)[:, 0, :]
    conv_b = np.asarray(inputs["L0_attn_conv_b"], np.float32)
    b_delta = np.asarray(inputs["L0_attn_b_delta"], np.float32)

    w_in_l = np.ascontiguousarray(w_in.reshape(DJ, 128, 32, 128).transpose(1, 0, 2, 3))
    w_delta_l = np.ascontiguousarray(
        w_delta.reshape(IJ, 128, IJ, 128).transpose(1, 0, 2, 3))
    w_bc_l = np.ascontiguousarray(
        np.concatenate([w_b, w_c], axis=1).reshape(IJ, 128, 32).transpose(1, 0, 2))
    lg = np.ascontiguousarray(ln_g.reshape(DJ, 128).T)
    lbm = np.ascontiguousarray(ln_b.reshape(DJ, 128).T)
    cwl = np.ascontiguousarray(conv_w.reshape(IJ, 128, 4).transpose(1, 0, 2))
    cbl = np.ascontiguousarray(conv_b.reshape(IJ, 128).T)
    bdl = np.ascontiguousarray(b_delta.reshape(IJ, 128).T)
    onec = np.full((128, 1), 1.0 / D, np.float32)
    oner = np.ones((1, 128), np.float32)

    maps = []
    for c in range(8):
        b, q = c // 4, c % 4
        xt = x[b].T
        sl = np.zeros((D, TOK + 3), np.float32)
        t0 = q * TOK
        if q == 0:
            sl[:, 3:] = xt[:, 0:TOK]
            hm = np.zeros((128, 1), np.float32)
        else:
            sl[:, :] = xt[:, t0 - 3:t0 + TOK]
            hm = np.ones((128, 1), np.float32)
        x_f = np.ascontiguousarray(sl.reshape(DJ, 128, TOK + 3).transpose(1, 0, 2))
        maps.append(dict(x_f=x_f, w_in=w_in_l, w_delta=w_delta_l, w_bc=w_bc_l,
                         ln_g=lg, ln_b=lbm, conv_w=cwl, conv_b=cbl, b_delta=bdl,
                         ones_col=onec, ones_row=oner, halo_mask=hm))
    return maps


def kernel_unfused(**inputs):
    cores = list(range(8))
    r1 = run_bass_kernel_spmd(build_l1(), host_l1_inputs(inputs), core_ids=cores)
    r2 = run_bass_kernel_spmd(build_l2(), host_l2_inputs(inputs, r1.results),
                              core_ids=cores)
    r3 = run_bass_kernel_spmd(build_l3(),
                              host_l3_inputs(inputs, r1.results, r2.results),
                              core_ids=cores)
    r4 = run_bass_kernel_spmd(build_l4(), host_l4_inputs(inputs, r3.results),
                              core_ids=cores)
    out = np.zeros((B, L, D), np.float32)
    for c in range(8):
        b, q = c // 4, c % 4
        o = r4.results[c]["o_out"]
        out[b, q * TOK:(q + 1) * TOK, :] = (
            o.transpose(1, 0, 2).reshape(D, TOK).T)
    return out


if __name__ == "__main__":
    pass



TC2 = 2048
NC2 = L // TC2


def build_l2(cx=None, io=None):
    if cx is None:
        nc = bass.Bass("TRN2", target_bir_lowering=False)
        delta_rs = nc.dram_tensor("delta_rs", [2, 2, 128, L], F32, kind="ExternalInput").ap()
        dx_rs = nc.dram_tensor("dx_rs", [2, 2, 128, L], F32, kind="ExternalInput").ap()
        bc_all = nc.dram_tensor("bc_all", [2, 32, L], F32, kind="ExternalInput").ap()
        y_rs = nc.dram_tensor("y_rs", [2, 2, 128, L], F32, kind="ExternalOutput").ap()
        io = {}
        io["log_a"] = nc.dram_tensor("log_a", [2, 128, S], F32, kind="ExternalInput").ap()
        io["sel_in"] = nc.dram_tensor("sel_in", [S, S, 128], F32, kind="ExternalInput").ap()
        io["ones_row2"] = nc.dram_tensor("ones_row", [1, 128], F32, kind="ExternalInput").ap()
        HT = TC2 // 2
        io["delta_slice"] = lambda b, db, t0, hh: delta_rs[b, db, :, t0 + hh * HT:t0 + (hh + 1) * HT]
        io["dx_slice"] = lambda b, db, t0, hh: dx_rs[b, db, :, t0 + hh * HT:t0 + (hh + 1) * HT]
        io["bcB_slice"] = lambda b, t0: bc_all[b, 0:S, t0:t0 + TC2]
        io["bcC_slice"] = lambda b, t0: bc_all[b, S:32, t0:t0 + TC2]
        io["y_slice"] = lambda b, db, t0, hh: y_rs[b, db, :, t0 + hh * HT:t0 + (hh + 1) * HT]
    else:
        nc = cx.nc
    log_a = io["log_a"]
    sel_in = io["sel_in"]
    ones_row = io["ones_row2"]

    es = ExitStack()
    sbt = lambda name, shape, dt=F32: es.enter_context(nc.sbuf_tensor("p2" + name, shape, dt))

    dlt = sbt("dlt", [128, TC2])
    dxt = sbt("dxt", [128, TC2])
    ab = sbt("ab", [128, 2, TC2])
    bx = sbt("bx", [128, TC2])
    ht = sbt("ht", [128, TC2])
    hc = sbt("hc", [128, 2, TC2])
    yt = sbt("yt", [128, 2, TC2])
    bccB = sbt("bccB", [S, TC2])
    bccC = sbt("bccC", [S, TC2])
    sel = sbt("sel", [S, S, 128])
    oner = sbt("oner", [1, 128])
    lat = sbt("lat", [128, 2, S])
    acol = sbt("acol", [128, 2, S])
    hcol = sbt("hcol", [128, 2, 2, S])

    ps_B = es.enter_context(nc.psum_tensor("p2ps_B", [128, TC2], F32))
    ps_C = es.enter_context(nc.psum_tensor("p2ps_C", [128, TC2], F32))

    with _phase_ctx(nc, cx) as (block, sems):
        sd, st, sv, ss, sg, so = sems

        @block.sync
        def _(sy):
            for db in range(2):
                sd.inc(sy.dma_start(out=lat[:, db, :], in_=log_a[db]), 16)
            sd.inc(sy.dma_start(out=oner[:], in_=ones_row), 16)
            sd.inc(sy.dma_start(out=sel[:], in_=sel_in), 16)

        @block.scalar
        def _(s):
            S_ = EC(s, ss)
            sd.wait(s)
            S_.activation(out=acol[:], in_=lat[:], func=AF.Exp)

        @block.vector
        def _(v):
            V = EC(v, sv)
            ss.wait(v)
            V.tensor_scalar(out=acol[:], in0=acol[:], scalar1=-1.0,
                            scalar2=None, op0=ALU.mult)

        for b in range(2):
            for c4 in range(NC2):
                t0 = c4 * TC2

                @block.sync
                def _(sy, b=b, t0=t0):
                    sv.wait(sy)
                    st.wait(sy)
                    sd.inc(sy.dma_start(out=bccB[:], in_=io["bcB_slice"](b, t0)), 16)
                    sd.inc(sy.dma_start(out=bccC[:], in_=io["bcC_slice"](b, t0)), 16)

                for db in range(2):
                    @block.sync
                    def _(sy, b=b, db=db, t0=t0):
                        sv.wait(sy)
                        ss.wait(sy)
                        for hh in range(2):
                            hsl = slice(hh * (TC2 // 2), (hh + 1) * (TC2 // 2))
                            sd.inc(sy.dma_start(out=dlt[:, hsl],
                                                in_=io["delta_slice"](b, db, t0, hh)), 16)
                            sd.inc(sy.dma_start(out=dxt[:, hsl],
                                                in_=io["dx_slice"](b, db, t0, hh)), 16)

                    for si in range(S):
                        pp = si % 2

                        @block.tensor
                        def _(t, b=b, si=si):
                            sd.wait(t)
                            sv.wait(t)
                            mm = None
                            for p in range(TC2 // CH):
                                t.matmul(ps_B[:, p * CH:(p + 1) * CH], sel[:, si, :],
                                         bccB[:, p * CH:(p + 1) * CH],
                                         start=True, stop=True)
                                mm = t.matmul(ps_C[:, p * CH:(p + 1) * CH],
                                              sel[:, si, :],
                                              bccC[:, p * CH:(p + 1) * CH],
                                              start=True, stop=True)
                            st.inc(mm)

                        @block.scalar
                        def _(s, db=db, si=si, pp=pp):
                            S_ = EC(s, ss)
                            sd.wait(s)
                            sv.wait(s)
                            S_.activation(out=ab[:, pp, :], in_=dlt[:], func=AF.Exp,
                                          scale=acol[:, db, si:si + 1])

                        @block.vector
                        def _(v, b=b, db=db, c4=c4, si=si, pp=pp):
                            V = EC(v, sv)
                            st.wait(v)
                            ss.wait(v)
                            sg.wait(v)
                            V.tensor_tensor(out=bx[:], in0=dxt[:], in1=ps_B[:, :],
                                            op=ALU.mult)
                            if c4 == 0:
                                V.tensor_tensor_scan(out=ht[:], data0=ab[:, pp, :],
                                                     data1=bx[:], initial=0.0,
                                                     op0=ALU.mult, op1=ALU.add)
                            else:
                                V.tensor_tensor_scan(out=ht[:], data0=ab[:, pp, :],
                                                     data1=bx[:],
                                                     initial=hcol[:, b, db, si:si + 1],
                                                     op0=ALU.mult, op1=ALU.add)
                            V.tensor_copy(out=hcol[:, b, db, si:si + 1],
                                          in_=ht[:, TC2 - 1:TC2])
                            V.tensor_tensor(out=hc[:, pp, :], in0=ht[:],
                                            in1=ps_C[:, :], op=ALU.mult)

                        @block.gpsimd
                        def _(g, si=si, pp=pp):
                            G = EC(g, sg)
                            sv.wait(g)
                            if si == 0:
                                so.wait(g)
                                G.tensor_copy(out=yt[:, 0, :], in_=hc[:, pp, :])
                            else:
                                G.tensor_tensor(out=yt[:, si % 2, :],
                                                in0=yt[:, (si + 1) % 2, :],
                                                in1=hc[:, pp, :], op=ALU.add)

                    @block.sync
                    def _(sy, b=b, db=db, t0=t0):
                        sg.wait(sy)
                        for hh in range(2):
                            hsl = slice(hh * (TC2 // 2), (hh + 1) * (TC2 // 2))
                            so.inc(sy.dma_start(out=io["y_slice"](b, db, t0, hh),
                                                in_=yt[:, (S - 1) % 2, hsl]), 16)

        @block.sync
        def _(sy):
            so.wait(sy)

    es.close()
    return nc


def host_l2_inputs(inputs, r1_results):
    log_A = np.asarray(inputs["L0_attn_log_A"], np.float32)
    delta_full = np.zeros((2, DI, L), np.float32)
    dx_full = np.zeros((2, DI, L), np.float32)
    bc_all = np.zeros((2, 32, L), np.float32)
    for c in range(8):
        b, q = c // 4, c % 4
        dd = r1_results[c]["o_delta"]
        dx = r1_results[c]["o_dx"]
        delta_full[b, :, q * TOK:(q + 1) * TOK] = dd.transpose(1, 0, 2).reshape(DI, TOK)
        dx_full[b, :, q * TOK:(q + 1) * TOK] = dx.transpose(1, 0, 2).reshape(DI, TOK)
        bc_all[b, :, q * TOK:(q + 1) * TOK] = r1_results[c]["o_bc"]
    oner = np.ones((1, 128), np.float32)
    sel = np.zeros((S, S, 128), np.float32)
    for si in range(S):
        sel[si, si, :] = 1.0
    maps = []
    for c in range(8):
        r0 = c * 256
        maps.append(dict(
            delta_rs=np.ascontiguousarray(
                delta_full[:, r0:r0 + 256, :].reshape(2, 2, 128, L)),
            dx_rs=np.ascontiguousarray(
                dx_full[:, r0:r0 + 256, :].reshape(2, 2, 128, L)),
            log_a=np.ascontiguousarray(log_A[r0:r0 + 256, :].reshape(2, 128, S)),
            bc_all=bc_all, sel_in=sel, ones_row=oner))
    return maps



def build_l3(cx=None, io=None):
    if cx is None:
        nc = bass.Bass("TRN2", target_bir_lowering=False)
        y_t = nc.dram_tensor("y_t", [128, IJ, TOK], F32, kind="ExternalInput").ap()
        io = {}
        io["x3"] = nc.dram_tensor("x_f", [128, DJ, TOK], F32, kind="ExternalInput").ap()
        io["xb3"] = nc.dram_tensor("xb_t", [128, IJ, TOK], F32, kind="ExternalInput").ap()
        io["z3"] = nc.dram_tensor("z_t", [128, IJ, TOK], F32, kind="ExternalInput").ap()
        io["dp"] = nc.dram_tensor("dp", [128, IJ], F32, kind="ExternalInput").ap()
        io["w_out"] = nc.dram_tensor("w_out", [128, IJ, DJ, 128], F32, kind="ExternalInput").ap()
        io["ones_col3"] = nc.dram_tensor("ones_col", [128, 1], F32, kind="ExternalInput").ap()
        io["ones_row3"] = nc.dram_tensor("ones_row", [1, 128], F32, kind="ExternalInput").ap()
        io["o_res1"] = nc.dram_tensor("o_res1", [128, DJ, TOK], F32, kind="ExternalOutput").ap()
        io["y3_slice"] = lambda m, c0: y_t[:, m, c0:c0 + CH]
    else:
        nc = cx.nc
    x_f = io["x3"]; xb_t = io["xb3"]; z_t = io["z3"]; dp = io["dp"]
    w_out = io["w_out"]; ones_col = io["ones_col3"]; ones_row = io["ones_row3"]
    o_res1 = io["o_res1"]

    es = ExitStack()
    sbt = lambda name, shape, dt=F32: es.enter_context(nc.sbuf_tensor("p3" + name, shape, dt))
    psb = lambda name, shape: es.enter_context(nc.psum_tensor("p3" + name, shape, F32))

    wq = sbt("wq", [128, IJ, DJ, 128], BF16)
    wstage = sbt("wstage", [128, DJ, 128], F32)
    dpc = sbt("dpc", [128, IJ])
    onec = sbt("onec", [128, 1]); oner = sbt("oner", [1, 128])
    kc = sbt("kc", [128, 4])
    rows = sbt("rows", [1, 4, CH])
    xt = sbt("xt", [128, DJ, CH])
    xn1 = sbt("xn1", [128, DJ, CH])
    sq = sbt("sq", [128, CH])
    ymini = sbt("ymini", [128, 2, CH])
    xbmini = sbt("xbmini", [128, 2, CH])
    zmini = sbt("zmini", [128, 2, CH])
    gz = sbt("gz", [128, 2, CH])
    ut = sbt("ut", [128, 2, CH])
    ugb = sbt("ugb", [128, IJ, CH], BF16)
    rstg = sbt("rstg", [128, DJ, CH])

    ps_mu = psb("ps_mu", [1, CH]); ps_m2 = psb("ps_m2", [1, CH])
    ps_b0 = psb("ps_b0", [128, CH]); ps_b1 = psb("ps_b1", [128, CH])
    ps_mm0 = psb("ps_mm0", [128, CH]); ps_mm1 = psb("ps_mm1", [128, CH])

    with _phase_ctx(nc, cx) as (block, sems):
        sd, st, sv, ss, sg, so = sems

        @block.vector
        def _(v):
            V = EC(v, sv)
            V.memset(kc[:, 0:1], EPS)
            V.memset(kc[:, 1:2], 1.0)

        @block.sync
        def _(sy):
            sd.inc(sy.dma_start(out=dpc[:], in_=dp), 16)
            sd.inc(sy.dma_start(out=onec[:], in_=ones_col), 16)
            sd.inc(sy.dma_start(out=oner[:], in_=ones_row), 16)

        for k in range(IJ):
            @block.sync
            def _(sy, k=k):
                sv.wait(sy)
                sd.inc(sy.dma_start(out=wstage[:], in_=w_out[:, k]), 16)

            @block.vector
            def _(v, k=k):
                V = EC(v, sv)
                sd.wait(v)
                V.tensor_copy(out=wq[:, k], in_=wstage[:])

        for ci in range(NCH):
            c0 = ci * CH

            @block.sync
            def _(sy, c0=c0):
                sv.wait(sy)
                ss.wait(sy)
                st.wait(sy)
                so.wait(sy)
                sd.inc(sy.dma_start(out=xt[:], in_=x_f[:, :, c0:c0 + CH]), 16)

            for j in range(DJ):
                @block.scalar
                def _(s, j=j):
                    S_ = EC(s, ss)
                    if j == 0:
                        sd.wait(s)
                    st.wait(s)
                    S_.activation(out=sq[:], in_=xt[:, j, :], func=AF.Square)

                @block.tensor
                def _(t, j=j):
                    ss.wait(t)
                    if j == 0:
                        sv.wait(t)
                    t.matmul(ps_mu[:, :], onec[:], xt[:, j, :],
                             start=(j == 0), stop=(j == DJ - 1))
                    st.inc(t.matmul(ps_m2[:, :], onec[:], sq[:],
                                    start=(j == 0), stop=(j == DJ - 1)))

            @block.vector
            def _(v):
                V = EC(v, sv)
                st.wait(v)
                V.tensor_copy(out=rows[:, 0, :], in_=ps_mu[:, :])
                V.tensor_copy(out=rows[:, 1, :], in_=ps_m2[:, :])
                V.tensor_tensor(out=rows[:, 2, :], in0=rows[:, 0, :],
                                in1=rows[:, 0, :], op=ALU.mult)
                V.tensor_tensor(out=rows[:, 2, :], in0=rows[:, 1, :],
                                in1=rows[:, 2, :], op=ALU.subtract)

            @block.scalar
            def _(s):
                S_ = EC(s, ss)
                sv.wait(s)
                S_.activation(out=rows[:, 3, :], in_=rows[:, 2, :],
                              func=AF.Sqrt, bias=kc[0:1, 0:1])

            @block.vector
            def _(v):
                V = EC(v, sv)
                ss.wait(v)
                V.reciprocal(out=rows[:, 2, :], in_=rows[:, 3, :])
                V.tensor_tensor(out=rows[:, 3, :], in0=rows[:, 0, :],
                                in1=rows[:, 2, :], op=ALU.mult)

            @block.tensor
            def _(t):
                sv.wait(t)
                t.matmul(ps_b1[:, :], oner[:], rows[:, 2, :], start=True, stop=True)
                st.inc(t.matmul(ps_b0[:, :], oner[:], rows[:, 3, :],
                                start=True, stop=True))

            @block.vector
            def _(v):
                V = EC(v, sv)
                st.wait(v)
                for j in range(DJ):
                    V.tensor_tensor(out=xn1[:, j, :], in0=xt[:, j, :],
                                    in1=ps_b1[:, :], op=ALU.mult)
                    V.tensor_tensor(out=xn1[:, j, :], in0=xn1[:, j, :],
                                    in1=ps_b0[:, :], op=ALU.subtract)
                    V.tensor_tensor(out=xn1[:, j, :], in0=xn1[:, j, :],
                                    in1=xt[:, j, :], op=ALU.add)

            mini_free = [sv.n, sv.n]
            gz_free = [sv.n, sv.n]
            for m in range(IJ):
                pp = m % 2

                @block.sync
                def _(sy, m=m, ci=ci, pp=pp, c0=c0, mv=mini_free[m % 2]):
                    sv.wait(sy, mv)
                    sd.inc(sy.dma_start(out=ymini[:, pp, :],
                                        in_=io["y3_slice"](m, c0)), 16)
                    sd.inc(sy.dma_start(out=xbmini[:, pp, :],
                                        in_=xb_t[:, m, c0:c0 + CH]), 16)
                    sd.inc(sy.dma_start(out=zmini[:, pp, :],
                                        in_=z_t[:, m, c0:c0 + CH]), 16)

                @block.scalar
                def _(s, m=m, pp=pp, mv=gz_free[m % 2]):
                    S_ = EC(s, ss)
                    sd.wait(s)
                    sv.wait(s, mv)
                    S_.activation(out=gz[:, pp, :], in_=zmini[:, pp, :], func=AF.Silu)

                @block.vector
                def _(v, m=m, pp=pp):
                    V = EC(v, sv)
                    sd.wait(v)
                    ss.wait(v)
                    V.scalar_tensor_tensor(out=ut[:, pp, :], in0=xbmini[:, pp, :],
                                           scalar=dpc[:, m:m + 1],
                                           in1=ymini[:, pp, :],
                                           op0=ALU.mult, op1=ALU.add)
                    V.tensor_tensor(out=ugb[:, m, :], in0=ut[:, pp, :],
                                    in1=gz[:, pp, :], op=ALU.mult)

                mini_free[pp] = sv.n
                gz_free[pp] = sv.n

            ug_ready = sv.n
            ps_free3 = [sv.n, sv.n]
            for m in range(DJ):
                ps = ps_mm0 if m % 2 == 0 else ps_mm1

                @block.tensor
                def _(t, m=m, ps=ps, mv=ps_free3[m % 2], mr=ug_ready):
                    sv.wait(t, max(mv, mr))
                    mm = None
                    for k in range(IJ):
                        mm = t.matmul(ps[:, :], wq[:, k, m], ugb[:, k, :],
                                      start=(k == 0), stop=(k == IJ - 1))
                    st.inc(mm)

                @block.vector
                def _(v, m=m, ps=ps):
                    V = EC(v, sv)
                    st.wait(v)
                    V.tensor_tensor(out=rstg[:, m, :], in0=xn1[:, m, :],
                                    in1=ps[:, :], op=ALU.add)

                ps_free3[m % 2] = sv.n

            @block.sync
            def _(sy, ci=ci):
                sv.wait(sy)
                so.inc(sy.dma_start(out=o_res1[:, :, ci * CH:(ci + 1) * CH],
                                    in_=rstg[:]), 16)

        @block.sync
        def _(sy):
            so.wait(sy)

    es.close()
    return nc


def host_l3_inputs(inputs, r1_results, r2_results):
    x = np.asarray(inputs["L0_attn_x"], np.float32)
    dp = np.asarray(inputs["L0_attn_Dp"], np.float32)
    w_out = np.asarray(inputs["L0_attn_W_out"], np.float32)

    y_full = np.zeros((2, DI, L), np.float32)
    for c in range(8):
        y_full[:, c * 256:(c + 1) * 256, :] = r2_results[c]["y_rs"].reshape(2, 256, L)

    w_out_l = np.ascontiguousarray(w_out.reshape(IJ, 128, DJ, 128).transpose(1, 0, 2, 3))
    dpl = np.ascontiguousarray(dp.reshape(IJ, 128).T)
    onec = np.full((128, 1), 1.0 / D, np.float32)
    oner = np.ones((1, 128), np.float32)

    maps = []
    for c in range(8):
        b, q = c // 4, c % 4
        xt = x[b].T[:, q * TOK:(q + 1) * TOK]
        x_f = np.ascontiguousarray(xt.reshape(DJ, 128, TOK).transpose(1, 0, 2))
        y_t = np.ascontiguousarray(
            y_full[b, :, q * TOK:(q + 1) * TOK].reshape(IJ, 128, TOK).transpose(1, 0, 2))
        maps.append(dict(x_f=x_f, y_t=y_t, xb_t=r1_results[c]["o_xb"],
                         z_t=r1_results[c]["o_z"], dp=dpl, w_out=w_out_l,
                         ones_col=onec, ones_row=oner))
    return maps



CH2 = 256
NCH2 = TOK // CH2


def build_l4(cx=None, io=None):
    if cx is None:
        nc = bass.Bass("TRN2", target_bir_lowering=False)
        io = {}
        io["res1_in"] = nc.dram_tensor("res1_in", [128, DJ, TOK], F32, kind="ExternalInput").ap()
        io["w1t"] = nc.dram_tensor("w1t", [128, DJ, HJ, 128], F32, kind="ExternalInput").ap()
        io["b1"] = nc.dram_tensor("b1", [128, HJ], F32, kind="ExternalInput").ap()
        io["w2t"] = nc.dram_tensor("w2t", [128, HJ, DJ, 128], F32, kind="ExternalInput").ap()
        io["b2"] = nc.dram_tensor("b2", [128, DJ], F32, kind="ExternalInput").ap()
        io["ones_col4"] = nc.dram_tensor("ones_col", [128, 1], F32, kind="ExternalInput").ap()
        io["ones_row4"] = nc.dram_tensor("ones_row", [1, 128], F32, kind="ExternalInput").ap()
        io["iden_in"] = nc.dram_tensor("iden_in", [128, 128], F32, kind="ExternalInput").ap()
        io["o_out"] = nc.dram_tensor("o_out", [128, DJ, TOK], F32, kind="ExternalOutput").ap()
    else:
        nc = cx.nc
    res1_in = io["res1_in"]; w1t = io["w1t"]; b1 = io["b1"]; w2t = io["w2t"]
    b2 = io["b2"]; ones_col = io["ones_col4"]; ones_row = io["ones_row4"]
    iden_in = io["iden_in"]; o_out = io["o_out"]

    es = ExitStack()
    sbt = lambda name, shape, dt=F32: es.enter_context(nc.sbuf_tensor("p4" + name, shape, dt))
    psb = lambda name, shape: es.enter_context(nc.psum_tensor("p4" + name, shape, F32))

    w1q = sbt("w1q", [128, DJ, HJ, 128], BF16)
    w2q = sbt("w2q", [128, HJ, DJ, 128], BF16)
    wstage = sbt("wstage", [128, 2, 1024], F32)
    b1c = sbt("b1c", [128, HJ]); b2c = sbt("b2c", [128, DJ])
    onec = sbt("onec", [128, 1]); oner = sbt("oner", [1, 128])
    iden = sbt("iden", [128, 128])
    kc = sbt("kc", [128, 4])
    wac = sbt("wac", [128, HJ])
    wrow = sbt("wrow", [1, 8])
    wcol = sbt("wcol", [128, 4])
    rows = sbt("rows", [1, 8, CH2])
    resc = sbt("resc", [128, DJ, CH2])
    xn3 = sbt("xn3", [128, DJ, CH2])
    sq2 = sbt("sq2", [128, CH2])
    cmx = sbt("cmx", [128, CH2])
    xst = sbt("xst", [128, 2])
    xqb = sbt("xqb", [128, DJ, CH2], BF16)
    q8 = sbt("q8", [128, CH2], I8)
    q8w = sbt("q8w", [128, 1024], I8)
    qf = sbt("qf", [128, 2, CH2])
    qfall = sbt("qfall", [128, 8, CH2])
    q8a = sbt("q8a", [128, 8 * CH2], I8)
    h1b = sbt("h1b", [128, HJ, CH2], BF16)
    dq_sb = sbt("dq_sb", [128, CH2])
    s_tok = sbt("s_tok", [128, 2])
    dq_tok = sbt("dq_tok", [128, 2])
    srow4 = sbt("srow4", [1, 512])
    dq2_sb = sbt("dq2_sb", [128, CH2])
    ostg = sbt("ostg", [128, 2, CH2])

    ps_mu = psb("ps_mu", [1, CH2]); ps_m2 = psb("ps_m2", [1, CH2])
    ps_b0 = psb("ps_b0", [128, CH2]); ps_b1 = psb("ps_b1", [128, CH2])
    ps_mm0 = psb("ps_mm0", [128, CH2]); ps_mm1 = psb("ps_mm1", [128, CH2])
    ps_tr = psb("ps_tr", [128, 512])
    ps_sc = psb("ps_sc", [128, 2])

    with _phase_ctx(nc, cx) as (block, sems):
        sd, st, sv, ss, sg, so = sems

        @block.vector
        def _(v):
            V = EC(v, sv)
            V.memset(kc[:, 0:1], EPS)
            V.memset(kc[:, 1:2], 1.0)
            V.memset(kc[:, 2:3], -1.49)
            V.memset(kc[:, 3:4], 1.49)


        @block.sync
        def _(sy):
            for t, src in [(b1c, b1), (b2c, b2), (onec, ones_col), (oner, ones_row),
                           (iden, iden_in)]:
                sd.inc(sy.dma_start(out=t[:], in_=src), 16)

        def emit_wprep(w_src, wq_dst, ws_i, wsr_i, nelem):
            wsf = w_src.rearrange("p a b c -> p (a b c)")
            wqf = wq_dst[:].rearrange("p a b c -> p (a b c)")
            total = wsf.shape[1]
            PW = 1024
            npiece = total // PW
            w_free = [sv.n, sv.n]
            for k in range(npiece):
                wp = k % 2

                @block.sync
                def _(sy, k=k, wp=wp, m=w_free[k % 2]):
                    sv.wait(sy, m)
                    ss.wait(sy)
                    sd.inc(sy.dma_start(out=wstage[:, wp, :],
                                        in_=wsf[:, k * PW:(k + 1) * PW]), 16)

                @block.vector
                def _(v, k=k, wp=wp):
                    V = EC(v, sv)
                    sd.wait(v)
                    V.tensor_reduce(out=wac[:, k:k + 1], in_=wstage[:, wp, :],
                                    axis=AX.X, op=ALU.add, apply_absolute_value=True)

                w_free[wp] = sv.n

            @block.vector
            def _(v, npiece=npiece):
                V = EC(v, sv)
                V.tensor_reduce(out=wac[:, 0:1], in_=wac[:, 0:npiece],
                                axis=AX.X, op=ALU.add)

            @block.tensor
            def _(t):
                sv.wait(t)
                st.inc(t.matmul(ps_sc[0:1, 0:1], kc[:, 1:2], wac[:, 0:1],
                                start=True, stop=True))

            @block.vector
            def _(v, ws_i=ws_i, wsr_i=wsr_i, nelem=nelem):
                V = EC(v, sv)
                st.wait(v)
                V.tensor_scalar(out=wrow[:, ws_i:ws_i + 1], in0=ps_sc[0:1, 0:1],
                                scalar1=1.0 / nelem, scalar2=EPS,
                                op0=ALU.mult, op1=ALU.max)
                V.reciprocal(out=wrow[:, wsr_i:wsr_i + 1],
                             in_=wrow[:, ws_i:ws_i + 1])

            @block.tensor
            def _(t, ws_i=ws_i, wsr_i=wsr_i):
                sv.wait(t)
                t.matmul(ps_sc[:, 0:1], oner[:], wrow[:, ws_i:ws_i + 1],
                         start=True, stop=True)
                st.inc(t.matmul(ps_sc[:, 1:2], oner[:], wrow[:, wsr_i:wsr_i + 1],
                                start=True, stop=True))

            @block.vector
            def _(v, ws_i=ws_i):
                V = EC(v, sv)
                st.wait(v)
                V.tensor_copy(out=wcol[:, 0:1], in_=ps_sc[:, 1:2])
                V.tensor_copy(out=wcol[:, 1 + ws_i // 2:2 + ws_i // 2],
                              in_=ps_sc[:, 0:1])

            w_free = [sv.n, sv.n]
            s_done = [ss.n, ss.n]
            for k in range(npiece):
                wp = k % 2

                @block.sync
                def _(sy, k=k, wp=wp, m=w_free[k % 2], m2=s_done[k % 2]):
                    sv.wait(sy, m)
                    ss.wait(sy, m2)
                    sd.inc(sy.dma_start(out=wstage[:, wp, :],
                                        in_=wsf[:, k * PW:(k + 1) * PW]), 16)

                @block.vector
                def _(v, k=k, wp=wp):
                    V = EC(v, sv)
                    sd.wait(v)
                    V.tensor_scalar(out=wstage[:, wp, :], in0=wstage[:, wp, :],
                                    scalar1=wcol[:, 0:1], scalar2=None,
                                    op0=ALU.mult)
                    V.tensor_scalar(out=wstage[:, wp, :], in0=wstage[:, wp, :],
                                    scalar1=kc[:, 2:3], scalar2=kc[:, 3:4],
                                    op0=ALU.max, op1=ALU.min)

                w_free[wp] = sv.n

                @block.scalar
                def _(s, k=k, wp=wp):
                    S_ = EC(s, ss)
                    sv.wait(s)
                    S_.copy(out=q8w[:], in_=wstage[:, wp, :])
                    S_.copy(out=wqf[:, k * PW:(k + 1) * PW], in_=q8w[:])

                s_done[wp] = ss.n

        emit_wprep(w1t, w1q, 0, 1, float(H * D))
        emit_wprep(w2t, w2q, 2, 3, float(H * D))

        resc_free = max(sv.n, 1)
        for ci in range(NCH2):
            c0 = ci * CH2

            @block.sync
            def _(sy, c0=c0, mv=resc_free):
                sv.wait(sy, mv)
                st.wait(sy)
                sd.inc(sy.dma_start(out=resc[:], in_=res1_in[:, :, c0:c0 + CH2]), 16)

            for j in range(DJ):
                @block.scalar
                def _(s, j=j):
                    S_ = EC(s, ss)
                    if j == 0:
                        sd.wait(s)
                    st.wait(s)
                    S_.activation(out=sq2[:], in_=resc[:, j, :], func=AF.Square)

                @block.tensor
                def _(t, j=j):
                    ss.wait(t)
                    if j == 0:
                        sv.wait(t)
                    t.matmul(ps_mu[:, :], onec[:], resc[:, j, :],
                             start=(j == 0), stop=(j == DJ - 1))
                    st.inc(t.matmul(ps_m2[:, :], onec[:], sq2[:],
                                    start=(j == 0), stop=(j == DJ - 1)))

            @block.vector
            def _(v):
                V = EC(v, sv)
                st.wait(v)
                V.tensor_copy(out=rows[:, 0, :], in_=ps_mu[:, :])
                V.tensor_copy(out=rows[:, 1, :], in_=ps_m2[:, :])
                V.tensor_tensor(out=rows[:, 2, :], in0=rows[:, 0, :],
                                in1=rows[:, 0, :], op=ALU.mult)
                V.tensor_tensor(out=rows[:, 2, :], in0=rows[:, 1, :],
                                in1=rows[:, 2, :], op=ALU.subtract)

            @block.scalar
            def _(s):
                S_ = EC(s, ss)
                sv.wait(s)
                S_.activation(out=rows[:, 3, :], in_=rows[:, 2, :],
                              func=AF.Sqrt, bias=kc[0:1, 0:1])

            @block.vector
            def _(v):
                V = EC(v, sv)
                ss.wait(v)
                V.reciprocal(out=rows[:, 2, :], in_=rows[:, 3, :])
                V.tensor_tensor(out=rows[:, 3, :], in0=rows[:, 0, :],
                                in1=rows[:, 2, :], op=ALU.mult)

            @block.tensor
            def _(t):
                sv.wait(t)
                t.matmul(ps_b1[:, :], oner[:], rows[:, 2, :], start=True, stop=True)
                st.inc(t.matmul(ps_b0[:, :], oner[:], rows[:, 3, :],
                                start=True, stop=True))

            @block.vector
            def _(v):
                V = EC(v, sv)
                st.wait(v)
                for j in range(DJ):
                    V.tensor_tensor(out=xn3[:, j, :], in0=resc[:, j, :],
                                    in1=ps_b1[:, :], op=ALU.mult)
                    V.tensor_tensor(out=xn3[:, j, :], in0=xn3[:, j, :],
                                    in1=ps_b0[:, :], op=ALU.subtract)

            def emit_quant(src, nj, qdst, ws_i, dqdst):
                @block.vector
                def _(v, src=src, nj=nj):
                    V = EC(v, sv)
                    ss.wait(v)
                    V.tensor_reduce(out=cmx[:], in_=src[:].rearrange("p j t -> p t j"),
                                    axis=AX.X, op=ALU.max, apply_absolute_value=True)

                @block.tensor
                def _(t):
                    sv.wait(t)
                    st.wait(t)
                    t.transpose(ps_tr[:, 0:128], cmx[:, 0:128], iden[:])
                    st.inc(t.transpose(ps_tr[:, 128:256], cmx[:, 128:256], iden[:]))

                @block.vector
                def _(v):
                    V = EC(v, sv)
                    st.wait(v)
                    V.tensor_reduce(out=xst[:, 0:1], in_=ps_tr[:, 0:128],
                                    axis=AX.X, op=ALU.max)
                    V.tensor_reduce(out=xst[:, 1:2], in_=ps_tr[:, 128:256],
                                    axis=AX.X, op=ALU.max)

                @block.vector
                def _(v, ws_i=ws_i):
                    V = EC(v, sv)
                    V.tensor_scalar(out=xst[:], in0=xst[:], scalar1=EPS,
                                    scalar2=None, op0=ALU.max)
                    V.reciprocal(out=s_tok[:], in_=xst[:])
                    V.tensor_scalar(out=s_tok[:], in0=s_tok[:],
                                    scalar1=127.0, scalar2=None, op0=ALU.mult)
                    V.tensor_scalar(out=dq_tok[:], in0=xst[:],
                                    scalar1=wcol[:, 1 + ws_i // 2:2 + ws_i // 2],
                                    scalar2=1.0 / 127.0, op0=ALU.mult, op1=ALU.mult)

                @block.tensor
                def _(t):
                    sv.wait(t)
                    t.transpose(ps_tr[0:1, 0:128], s_tok[:, 0:1], iden[:])
                    t.transpose(ps_tr[0:1, 128:256], s_tok[:, 1:2], iden[:])
                    t.transpose(ps_tr[0:1, 256:384], dq_tok[:, 0:1], iden[:])
                    st.inc(t.transpose(ps_tr[0:1, 384:512], dq_tok[:, 1:2], iden[:]))

                @block.vector
                def _(v):
                    V = EC(v, sv)
                    st.wait(v)
                    V.tensor_copy(out=srow4[:], in_=ps_tr[0:1, :])

                @block.tensor
                def _(t):
                    sv.wait(t)
                    t.matmul(ps_b0[:, 0:128], oner[:], srow4[:, 0:128],
                             start=True, stop=True)
                    t.matmul(ps_b0[:, 128:256], oner[:], srow4[:, 128:256],
                             start=True, stop=True)
                    t.matmul(ps_b1[:, 0:128], oner[:], srow4[:, 256:384],
                             start=True, stop=True)
                    st.inc(t.matmul(ps_b1[:, 128:256], oner[:],
                                    srow4[:, 384:512], start=True, stop=True))

                @block.vector
                def _(v, dqdst=dqdst):
                    V = EC(v, sv)
                    st.wait(v)
                    V.tensor_copy(out=dqdst[:], in_=ps_b1[:, :])

                qa_free = ss.n
                for j0 in range(0, nj, 8):
                    @block.vector
                    def _(v, j0=j0, src=src, m=qa_free):
                        V = EC(v, sv)
                        ss.wait(v, m)
                        for j in range(j0, j0 + 8):
                            V.tensor_tensor(out=qfall[:, j - j0, :], in0=src[:, j, :],
                                            in1=ps_b0[:, :], op=ALU.mult)

                    @block.scalar
                    def _(s, j0=j0, qdst=qdst):
                        S_ = EC(s, ss)
                        sv.wait(s)
                        S_.copy(out=q8a[:], in_=qfall[:].rearrange("p a b -> p (a b)"))
                        S_.copy(out=qdst[:, j0:j0 + 8, :].rearrange("p a b -> p (a b)"),
                                in_=q8a[:])

                    qa_free = ss.n

            emit_quant(xn3, DJ, xqb, 0, dq_sb)

            xqb_ready = ss.n
            ps_free2 = [sv.n, sv.n]
            qa_free = ss.n
            for m in range(HJ):
                ps = ps_mm0 if m % 2 == 0 else ps_mm1

                @block.tensor
                def _(t, m=m, ps=ps, mv=ps_free2[m % 2], ms=xqb_ready):
                    ss.wait(t, ms)
                    sv.wait(t, mv)
                    mm = None
                    for k in range(DJ):
                        mm = t.matmul(ps[:, :], w1q[:, k, m], xqb[:, k, :],
                                      start=(k == 0), stop=(k == DJ - 1))
                    st.inc(mm)

                @block.vector
                def _(v, m=m, ps=ps, mq=qa_free):
                    V = EC(v, sv)
                    st.wait(v)
                    ss.wait(v, mq)
                    V.tensor_tensor(out=qfall[:, m % 8, :], in0=ps[:, :],
                                    in1=dq_sb[:], op=ALU.mult)
                    V.tensor_scalar(out=qfall[:, m % 8, :], in0=qfall[:, m % 8, :],
                                    scalar1=b1c[:, m:m + 1], scalar2=None,
                                    op0=ALU.add)

                ps_free2[m % 2] = sv.n

                if m % 8 == 7:
                    @block.scalar
                    def _(s, m0=m - 7):
                        S_ = EC(s, ss)
                        sv.wait(s)
                        S_.activation(
                            out=h1b[:, m0:m0 + 8, :].rearrange("p a b -> p (a b)"),
                            in_=qfall[:].rearrange("p a b -> p (a b)"),
                            func=AF.Gelu)

                    qa_free = ss.n

            emit_quant(h1b, HJ, h1b, 2, dq2_sb)

            hq_ready = ss.n
            ps_free2 = [sv.n, sv.n]
            o_free = [so.n, so.n]
            for m in range(DJ):
                ps = ps_mm0 if m % 2 == 0 else ps_mm1

                @block.tensor
                def _(t, m=m, ps=ps, mv=ps_free2[m % 2], ms=hq_ready):
                    ss.wait(t, ms)
                    sv.wait(t, mv)
                    mm = None
                    for k in range(HJ):
                        mm = t.matmul(ps[:, :], w2q[:, k, m], h1b[:, k, :],
                                      start=(k == 0), stop=(k == HJ - 1))
                    st.inc(mm)

                @block.vector
                def _(v, m=m, ps=ps, mo=o_free[m % 2]):
                    V = EC(v, sv)
                    st.wait(v)
                    so.wait(v, mo)
                    V.tensor_tensor(out=qf[:, m % 2, :], in0=ps[:, :], in1=dq2_sb[:],
                                    op=ALU.mult)
                    V.scalar_tensor_tensor(out=ostg[:, m % 2, :], in0=qf[:, m % 2, :],
                                           scalar=b2c[:, m:m + 1],
                                           in1=resc[:, m, :],
                                           op0=ALU.add, op1=ALU.add)

                ps_free2[m % 2] = sv.n

                @block.sync
                def _(sy, m=m, ci=ci):
                    sv.wait(sy)
                    so.inc(sy.dma_start(
                        out=o_out[:, m, ci * CH2:(ci + 1) * CH2],
                        in_=ostg[:, m % 2, :]), 16)

                o_free[m % 2] = so.n

            resc_free = sv.n

        @block.sync
        def _(sy):
            so.wait(sy)

    es.close()
    return nc


def host_l4_inputs(inputs, r3_results):
    w1 = np.asarray(inputs["L0_ffn_W1"], np.float32)
    b1 = np.asarray(inputs["L0_ffn_b1"], np.float32)
    w2 = np.asarray(inputs["L0_ffn_W2"], np.float32)
    b2 = np.asarray(inputs["L0_ffn_b2"], np.float32)

    w1t = np.ascontiguousarray(w1.T.reshape(DJ, 128, HJ, 128).transpose(1, 0, 2, 3))
    w2t = np.ascontiguousarray(w2.T.reshape(HJ, 128, DJ, 128).transpose(1, 0, 2, 3))
    b1l = np.ascontiguousarray(b1.reshape(HJ, 128).T)
    b2l = np.ascontiguousarray(b2.reshape(DJ, 128).T)
    onec = np.full((128, 1), 1.0 / D, np.float32)
    oner = np.ones((1, 128), np.float32)
    iden = np.eye(128, dtype=np.float32)

    maps = []
    for c in range(8):
        maps.append(dict(res1_in=r3_results[c]["o_res1"], w1t=w1t, b1=b1l,
                         w2t=w2t, b2=b2l, ones_col=onec, ones_row=oner,
                         iden_in=iden))
    return maps



def build_fused():
    nc = bass.Bass("TRN2", target_bir_lowering=False)
    ein = lambda n, s, dt=F32: nc.dram_tensor(n, s, dt, kind="ExternalInput").ap()
    itn = lambda n, s, dt=F32: nc.dram_tensor(n, s, dt, kind="Internal").ap()

    x_f = ein("x_f", [128, DJ, TOK + 3])
    w_in = ein("w_in", [128, DJ, 32, 128])
    w_delta = ein("w_delta", [128, IJ, IJ, 128])
    w_bc = ein("w_bc", [128, IJ, 32])
    ln_g = ein("ln_g", [128, DJ]); ln_b = ein("ln_b", [128, DJ])
    conv_w = ein("conv_w", [128, IJ, 4]); conv_b = ein("conv_b", [128, IJ])
    b_delta = ein("b_delta", [128, IJ])
    ones_col = ein("ones_col", [128, 1]); ones_row = ein("ones_row", [1, 128])
    halo_mask = ein("halo_mask", [128, 1])
    log_a = ein("log_a", [128, IJ, S])
    m1 = ein("m1", [128, 64])
    m2 = ein("m2", [128, 8])
    sel_in = ein("sel_in", [S, S, 128], BF16)
    dp = ein("dp", [128, IJ])
    w_out = ein("w_out", [128, IJ, DJ, 128])
    w1t = ein("w1t", [128, DJ, HJ, 128]); b1 = ein("b1", [128, HJ])
    w2t = ein("w2t", [128, HJ, DJ, 128]); b2 = ein("b2", [128, DJ])
    iden_in = ein("iden_in", [128, 128])
    o_out = nc.dram_tensor("o_out", [128, DJ, TOK], F32, kind="ExternalOutput").ap()

    d_hbm = itn("d_hbm", [128, IJ, TOK])
    dx_hbm = itn("dx_hbm", [128, IJ, TOK])
    xb_hbm = itn("xb_hbm", [128, IJ, TOK])
    z_hbm = itn("z_hbm", [128, IJ, TOK])
    bc_hbm = itn("bc_hbm", [32, TOK], BF16)
    xbb_hbm = itn("xbb_hbm", [128, IJ, TOK], BF16)
    y0_hbm = itn("y0_hbm", [128, IJ, TOK])
    y_full = itn("y_full", [128, IJ, TOK])
    states_send = itn("states_send", [128, IJ, 2 * S])
    states_gath = itn("states_gath", [8, 128, IJ, 2 * S])
    res1_hbm = itn("res1_hbm", [128, DJ, TOK])

    with (
        nc.semaphore("s_d") as _sd,
        nc.semaphore("s_t") as _st,
        nc.semaphore("s_v") as _sv,
        nc.semaphore("s_s") as _ss,
        nc.semaphore("s_g") as _sg,
        nc.semaphore("s_o") as _so,
        nc.semaphore("s_cc") as _cc,
        nc.Block() as block,
    ):
        sems = (Sem(_sd), Sem(_st), Sem(_sv), Sem(_ss), Sem(_sg), Sem(_so))
        sd, st, sv, ss, sg, so = sems
        cx = Ctx(nc, block, sems)

        io1 = dict(x_f=x_f, w_in=w_in, w_delta=w_delta, w_bc=w_bc, ln_g=ln_g,
                   ln_b=ln_b, conv_w=conv_w, conv_b=conv_b, b_delta=b_delta,
                   ones_col=ones_col, ones_row=ones_row, halo_mask=halo_mask,
                   o_xb=xb_hbm, o_z=z_hbm, o_bc=bc_hbm, xbb_hbm=xbb_hbm,
                   odelta_slice=lambda m, ci: d_hbm[:, m, ci * CH:(ci + 1) * CH],
                   odx_slice=lambda m, ci: dx_hbm[:, m, ci * CH:(ci + 1) * CH])
        build_l1(cx, io1)

        emit_fence(block, sems)

        io2 = dict(d_hbm=d_hbm, dx_hbm=dx_hbm, bc_hbm=bc_hbm, log_a=log_a,
                   sel_in=sel_in, m1=m1, m2=m2, y0_hbm=y0_hbm, y_full=y_full,
                   states_send=states_send, states_gath=states_gath, cc_sem=_cc)
        emit_p2(cx, io2)

        emit_fence(block, sems)

        io3 = dict(
            x3=x_f[:, :, 3:3 + TOK], xb3=xb_hbm, z3=z_hbm, dp=dp, w_out=w_out,
            ones_col3=ones_col, ones_row3=ones_row, o_res1=res1_hbm,
            y3_slice=lambda m, c0: y_full[:, m, c0:c0 + CH])
        build_l3(cx, io3)

        emit_fence(block, sems)

        io4 = dict(res1_in=res1_hbm, w1t=w1t, b1=b1, w2t=w2t, b2=b2,
                   ones_col4=ones_col, ones_row4=ones_row, iden_in=iden_in,
                   o_out=o_out)
        build_l4(cx, io4)

        @block.sync
        def _(sy):
            so.wait(sy)

    return nc


def host_fused_inputs(inputs):
    maps1 = host_l1_inputs(inputs)
    log_A = np.asarray(inputs["L0_attn_log_A"], np.float32)
    dpv = np.asarray(inputs["L0_attn_Dp"], np.float32)
    w_out = np.asarray(inputs["L0_attn_W_out"], np.float32)
    w1 = np.asarray(inputs["L0_ffn_W1"], np.float32)
    b1 = np.asarray(inputs["L0_ffn_b1"], np.float32)
    w2 = np.asarray(inputs["L0_ffn_W2"], np.float32)
    b2 = np.asarray(inputs["L0_ffn_b2"], np.float32)

    w_out_l = np.ascontiguousarray(w_out.reshape(IJ, 128, DJ, 128).transpose(1, 0, 2, 3))
    dpl = np.ascontiguousarray(dpv.reshape(IJ, 128).T)
    w1t = np.ascontiguousarray(w1.T.reshape(DJ, 128, HJ, 128).transpose(1, 0, 2, 3))
    w2t = np.ascontiguousarray(w2.T.reshape(HJ, 128, DJ, 128).transpose(1, 0, 2, 3))
    b1l = np.ascontiguousarray(b1.reshape(HJ, 128).T)
    b2l = np.ascontiguousarray(b2.reshape(DJ, 128).T)
    import ml_dtypes
    iden = np.eye(128, dtype=np.float32)
    sel = np.zeros((S, S, 128), ml_dtypes.bfloat16)
    for si in range(S):
        sel[si, si, :] = 1.0

    log_a_full = np.ascontiguousarray(log_A.reshape(IJ, 128, S).transpose(1, 0, 2))
    maps = []
    for c in range(8):
        m = dict(maps1[c])
        b, q = c // 4, c % 4
        m1m = np.zeros((8, 8), np.float32)
        m2m = np.zeros((8,), np.float32)
        for r in range(8):
            rb, rq = r // 4, r % 4
            if rb == b and rq < q:
                m2m[r] = 1.0
                for k in range(8):
                    kb, kq = k // 4, k % 4
                    if kb == b and rq < kq < q:
                        m1m[k, r] = 1.0
        m.update(log_a=log_a_full, sel_in=sel, dp=dpl, w_out=w_out_l, w1t=w1t,
                 b1=b1l, w2t=w2t, b2=b2l, iden_in=iden,
                 m1=np.ascontiguousarray(np.tile(m1m.reshape(1, 64), (128, 1))),
                 m2=np.ascontiguousarray(np.tile(m2m.reshape(1, 8), (128, 1))))
        maps.append(m)
    return maps


def kernel(**inputs):
    cores = list(range(8))
    nc = build_fused()
    r = run_bass_kernel_spmd(nc, host_fused_inputs(inputs), core_ids=cores)
    out = np.zeros((B, L, D), np.float32)
    for c in range(8):
        b, q = c // 4, c % 4
        o = r.results[c]["o_out"]
        out[b, q * TOK:(q + 1) * TOK, :] = o.transpose(1, 0, 2).reshape(D, TOK).T
    return out





GG = 4


def emit_p2(cx, io):
    nc = cx.nc
    block = cx.block
    sd, st, sv, ss, sg, so = cx.sems
    _cc = io["cc_sem"]

    es = ExitStack()
    sbt = lambda name, shape, dt=F32: es.enter_context(nc.sbuf_tensor("s2" + name, shape, dt))

    dl4 = sbt("dl4", [128, GG, TOK])
    dx4 = sbt("dx4", [128, GG, TOK])
    ab = sbt("ab", [128, 2, TOK])
    bx = sbt("bx", [128, 2, TOK], BF16)
    bsb = sbt("bsb", [128, 2, TOK], BF16)
    one_t = sbt("one_t", [128, TOK], BF16)
    h0 = sbt("h0", [128, 2, TOK])
    hc = sbt("hc", [128, 2, TOK])
    ypp = sbt("ypp", [128, GG, TOK])
    bccB = sbt("bccB", [S, TOK], BF16)
    bccC = sbt("bccC", [S, TOK], BF16)
    sel = sbt("sel", [S, S, 128], BF16)
    lat = sbt("lat", [128, IJ, S])
    acol = sbt("acol", [128, IJ, S])
    hend = sbt("hend", [128, IJ, S])
    dlog = sbt("dlog", [128, IJ, S])
    pn = sbt("pn", [128, 1])
    hin = sbt("hin", [128, IJ, S])
    m1t = sbt("m1t", [128, 64])
    m2t = sbt("m2t", [128, 8])
    sgt = sbt("sgt", [128, 8, IJ, 2 * S])
    lacc = sbt("lacc", [128, 2, IJ, S])
    hacc = sbt("hacc", [128, 2, IJ, S])
    wr = sbt("wr", [128, IJ, S])

    ps_B = es.enter_context(nc.psum_tensor("s2ps_B", [128, TOK], F32))
    ps_C = es.enter_context(nc.psum_tensor("s2ps_C", [128, TOK], F32))

    @block.sync
    def _(sy):
        so.wait(sy)
        sv.wait(sy)
        ss.wait(sy)
        st.wait(sy)
        sd.inc(sy.dma_start(out=lat[:], in_=io["log_a"]), 16)
        sd.inc(sy.dma_start(out=sel[:], in_=io["sel_in"]), 16)
        sd.inc(sy.dma_start(out=m1t[:], in_=io["m1"]), 16)
        sd.inc(sy.dma_start(out=m2t[:], in_=io["m2"]), 16)
        sd.inc(sy.dma_start(out=bccB[:], in_=io["bc_hbm"][0:S, :]), 16)
        sd.inc(sy.dma_start(out=bccC[:], in_=io["bc_hbm"][S:32, :]), 16)

    @block.scalar
    def _(s):
        S_ = EC(s, ss)
        sd.wait(s)
        S_.activation(out=acol[:], in_=lat[:], func=AF.Exp)

    @block.vector
    def _(v):
        V = EC(v, sv)
        ss.wait(v)
        V.tensor_scalar(out=acol[:], in0=acol[:], scalar1=-1.0, scalar2=None,
                        op0=ALU.mult)
        V.memset(one_t[:], 1.0)

    for gg in range(IJ // GG):
        @block.sync
        def _(sy, gg=gg):
            sv.wait(sy)
            ss.wait(sy)
            so.wait(sy)
            sd.inc(sy.dma_start(out=dl4[:],
                                in_=io["d_hbm"][:, gg * GG:(gg + 1) * GG, :]), 16)
            sd.inc(sy.dma_start(out=dx4[:],
                                in_=io["dx_hbm"][:, gg * GG:(gg + 1) * GG, :]), 16)

        for gl in range(GG):
            g = gg * GG + gl

            @block.vector
            def _(v, g=g, gl=gl):
                V = EC(v, sv)
                sd.wait(v)
                ss.wait(v)
                sg.wait(v)
                V.tensor_tensor_scan(out=h0[:, 0, :], data0=one_t[:],
                                     data1=dl4[:, gl, :], initial=0.0,
                                     op0=ALU.mult, op1=ALU.add)
                V.tensor_copy(out=pn[:], in_=h0[:, 0, TOK - 1:TOK])
                V.tensor_scalar(out=dlog[:, g, :], in0=acol[:, g, :],
                                scalar1=pn[:], scalar2=None, op0=ALU.mult)

        ab_free = [sv.n, sv.n]
        hc_free = [sg.n, sg.n]
        bsb_free = [sg.n, sg.n]
        psC_free = sv.n
        bxr = {}

        def emit_bcastB(si_):
            @block.tensor
            def _(t, si_=si_, m=psB_free[0]):
                sd.wait(t)
                ss.wait(t, m)
                mm = None
                for p in range(TOK // CH):
                    mm = t.matmul(ps_B[:, p * CH:(p + 1) * CH], sel[:, si_, :],
                                  bccB[:, p * CH:(p + 1) * CH],
                                  start=True, stop=True)
                st.inc(mm)

            @block.scalar
            def _(s, si_=si_, m=bsb_free[si_ % 2]):
                S_ = EC(s, ss)
                st.wait(s)
                sg.wait(s, m)
                S_.copy(out=bsb[:, si_ % 2, :], in_=ps_B[:, :])

            psB_free[0] = ss.n

        psB_free = [ss.n]
        emit_bcastB(0)

        @block.gpsimd
        def _(gp):
            G = EC(gp, sg)
            sd.wait(gp)
            ss.wait(gp)
            sv.wait(gp)
            G.tensor_tensor(out=bx[:, 0, :], in0=dx4[:, 0, :],
                            in1=bsb[:, 0, :], op=ALU.mult)
            G.tensor_tensor(out=bx[:, 1, :], in0=dx4[:, 1, :],
                            in1=bsb[:, 0, :], op=ALU.mult)

        bxr[0] = sg.n
        bxr[1] = sg.n

        for si in range(S):
            @block.tensor
            def _(t, si=si, pv=psC_free):
                sd.wait(t)
                sv.wait(t, pv)
                mm = None
                for p in range(TOK // CH):
                    mm = t.matmul(ps_C[:, p * CH:(p + 1) * CH], sel[:, si, :],
                                  bccC[:, p * CH:(p + 1) * CH],
                                  start=True, stop=True)
                st.inc(mm)

            for gl in range(GG):
                g = gg * GG + gl
                it = si * GG + gl
                hpp = it % 2

                @block.scalar
                def _(s, g=g, gl=gl, hpp=hpp, m=ab_free[hpp]):
                    S_ = EC(s, ss)
                    sd.wait(s)
                    sv.wait(s, m)
                    S_.activation(out=ab[:, hpp, :], in_=dl4[:, gl, :], func=AF.Exp,
                                  scale=acol[:, g, si:si + 1])

                @block.vector
                def _(v, g=g, si=si, hpp=hpp,
                      hm=max(hc_free[hpp], bxr[it]), ):
                    V = EC(v, sv)
                    st.wait(v)
                    ss.wait(v)
                    sg.wait(v, hm)
                    V.tensor_tensor_scan(out=h0[:, 0, :], data0=ab[:, hpp, :],
                                         data1=bx[:, hpp, :], initial=0.0,
                                         op0=ALU.mult, op1=ALU.add)
                    V.tensor_copy(out=hend[:, g, si:si + 1],
                                  in_=h0[:, 0, TOK - 1:TOK])
                    V.tensor_tensor(out=hc[:, hpp, :], in0=h0[:, 0, :],
                                    in1=ps_C[:, :], op=ALU.mult)

                ab_free[hpp] = sv.n
                if gl == GG - 1:
                    psC_free = sv.n

                if gl == 0 and si < S - 1:
                    emit_bcastB(si + 1)

                @block.gpsimd
                def _(gp, si=si, gl=gl, it=it, hpp=hpp):
                    G = EC(gp, sg)
                    sv.wait(gp)
                    if si == 0:
                        so.wait(gp)
                        G.tensor_copy(out=ypp[:, gl, :], in_=hc[:, hpp, :])
                    else:
                        G.tensor_tensor(out=ypp[:, gl, :], in0=ypp[:, gl, :],
                                        in1=hc[:, hpp, :], op=ALU.add)
                    i2 = it + 2
                    if i2 < S * GG:
                        si2, gl2 = i2 // GG, i2 % GG
                        G.tensor_tensor(out=bx[:, i2 % 2, :],
                                        in0=dx4[:, gl2, :],
                                        in1=bsb[:, si2 % 2, :], op=ALU.mult)

                hc_free[hpp] = sg.n
                bxr[it + 2] = sg.n
                if gl == 1:
                    bsb_free[si % 2] = sg.n

        @block.sync
        def _(sy, gg=gg):
            sg.wait(sy)
            sv.wait(sy)
            for gl in range(GG):
                so.inc(sy.dma_start(out=io["y0_hbm"][:, gg * GG + gl, :],
                                    in_=ypp[:, gl, :]), 16)

    @block.sync
    def _(sy):
        sv.wait(sy)
        so.inc(sy.dma_start(out=io["states_send"][:, :, 0:S], in_=dlog[:]), 16)
        so.inc(sy.dma_start(out=io["states_send"][:, :, S:2 * S], in_=hend[:]), 16)

    @block.gpsimd
    def _(gp):
        so.wait(gp)
        sg.wait(gp)
        gp.collective_compute("AllGather", ALU.bypass,
                              replica_groups=[list(range(8))],
                              ins=[io["states_send"].opt()],
                              outs=[io["states_gath"].opt()]).then_inc(_cc, 1)

    @block.sync
    def _(sy):
        sy.wait_ge(_cc, 1)
        sd.inc(sy.dma_start(out=sgt[:], in_=io["states_gath"].rearrange("r p a b -> p r a b")), 16)

    @block.vector
    def _(v):
        V = EC(v, sv)
        sd.wait(v)
        V.memset(lacc[:, 0], 0.0)
        V.memset(hacc[:, 0], 0.0)

    for r in range(8):
        @block.vector
        def _(v, r=r):
            V = EC(v, sv)
            for k in range(8):
                V.scalar_tensor_tensor(
                    out=lacc[:, (k + 1) % 2],
                    in0=sgt[:, k, :, 0:S],
                    scalar=m1t[:, k * 8 + r:k * 8 + r + 1],
                    in1=lacc[:, k % 2], op0=ALU.mult, op1=ALU.add)

        @block.scalar
        def _(s, r=r):
            S_ = EC(s, ss)
            sv.wait(s)
            S_.activation(out=wr[:], in_=lacc[:, 0], func=AF.Exp)

        @block.vector
        def _(v, r=r):
            V = EC(v, sv)
            ss.wait(v)
            V.scalar_tensor_tensor(
                out=wr[:], in0=wr[:], scalar=m2t[:, r:r + 1],
                in1=sgt[:, r, :, S:2 * S],
                op0=ALU.mult, op1=ALU.mult)
            V.tensor_tensor(out=hacc[:, (r + 1) % 2], in0=hacc[:, r % 2],
                            in1=wr[:], op=ALU.add)
            if r < 7:
                V.memset(lacc[:, 0], 0.0)

    @block.vector
    def _(v):
        V = EC(v, sv)
        V.tensor_copy(out=hin[:], in_=hacc[:, 0])

    VQ2 = 1024
    for gg in range(IJ // GG):
        @block.sync
        def _(sy, gg=gg):
            sv.wait(sy)
            ss.wait(sy)
            so.wait(sy)
            sd.inc(sy.dma_start(out=dl4[:],
                                in_=io["d_hbm"][:, gg * GG:(gg + 1) * GG, :]), 16)

        for gl in range(GG):
            @block.vector
            def _(v, gl=gl):
                V = EC(v, sv)
                sd.wait(v)
                V.tensor_tensor_scan(out=dx4[:, gl, :], data0=one_t[:],
                                     data1=dl4[:, gl, :], initial=0.0,
                                     op0=ALU.mult, op1=ALU.add)

        ab_free = [sv.n, sv.n]
        hc_free = [sg.n, sg.n]
        ps_free = sv.n
        for si in range(S):
            @block.tensor
            def _(t, si=si, ps_free=ps_free):
                sd.wait(t)
                sv.wait(t, ps_free)
                mm = None
                for p in range(TOK // CH):
                    mm = t.matmul(ps_C[:, p * CH:(p + 1) * CH], sel[:, si, :],
                                  bccC[:, p * CH:(p + 1) * CH], start=True,
                                  stop=True)
                st.inc(mm)

            for gl in range(GG):
                g = gg * GG + gl
                hpp = (si * GG + gl) % 2

                @block.scalar
                def _(s, g=g, gl=gl, si=si, hpp=hpp, m=ab_free[(si * GG + gl) % 2]):
                    S_ = EC(s, ss)
                    sv.wait(s, m)
                    S_.activation(out=ab[:, hpp, :], in_=dx4[:, gl, :], func=AF.Exp,
                                  scale=acol[:, g, si:si + 1])

                @block.vector
                def _(v, g=g, gl=gl, si=si, hpp=hpp):
                    V = EC(v, sv)
                    st.wait(v)
                    ss.wait(v)
                    if si == 0:
                        so.wait(v)
                        V.scalar_tensor_tensor(out=ypp[:, gl, :],
                                               in0=ab[:, hpp, :],
                                               scalar=hin[:, g, si:si + 1],
                                               in1=ps_C[:, :], op0=ALU.mult,
                                               op1=ALU.mult)
                    else:
                        V.scalar_tensor_tensor(out=hc[:, 0, :], in0=ab[:, hpp, :],
                                               scalar=hin[:, g, si:si + 1],
                                               in1=ps_C[:, :], op0=ALU.mult,
                                               op1=ALU.mult)
                        V.tensor_tensor(out=ypp[:, gl, :], in0=ypp[:, gl, :],
                                        in1=hc[:, 0, :], op=ALU.add)

                ab_free[hpp] = sv.n
                if gl == GG - 1:
                    ps_free = sv.n

        for gl in range(GG):
            g = gg * GG + gl

            @block.sync
            def _(sy, g=g):
                sg.wait(sy)
                sv.wait(sy)
                sd.inc(sy.dma_start(out=hc[:, 1, :], in_=io["y0_hbm"][:, g, :]), 16)

            @block.vector
            def _(v, g=g, gl=gl):
                V = EC(v, sv)
                sd.wait(v)
                sg.wait(v)
                so.wait(v)
                V.tensor_tensor(out=h0[:, 0, :], in0=hc[:, 1, :], in1=ypp[:, gl, :],
                                op=ALU.add)

            @block.sync
            def _(sy, g=g):
                sv.wait(sy)
                so.inc(sy.dma_start(out=io["y_full"][:, g, :], in_=h0[:, 0, :]), 16)

    es.close()


# version 32
# speedup vs baseline: 1.0196x; 1.0196x over previous
from contextlib import ExitStack

import numpy as np
import concourse.bass as bass
import concourse.mybir as mybir
from concourse.bass_utils import run_bass_kernel_spmd

F32 = mybir.dt.float32
BF16 = mybir.dt.bfloat16
I8 = mybir.dt.int8
AF = mybir.ActivationFunctionType
ALU = mybir.AluOpType
AX = mybir.AxisListType

B, L, D, DI, S, H = 2, 8192, 1024, 2048, 16, 4096
TOK = 2048
CH = 512
NCH = TOK // CH
EPS = 1e-5
DJ = D // 128
IJ = DI // 128
HJ = H // 128


class Sem:
    def __init__(self, h):
        self.h = h
        self.n = 0

    def inc(self, instr, k=1):
        instr.then_inc(self.h, k)
        self.n += k
        return self.n

    def wait(self, eng, n=None):
        eng.wait_ge(self.h, self.n if n is None else n)


class EC:

    def __init__(self, eng, sem):
        self._e = eng
        self._s = sem

    def __getattr__(self, name):
        f = getattr(self._e, name)

        def g(*a, **kw):
            self._s.wait(self._e)
            ins = f(*a, **kw)
            self._s.inc(ins)
            return ins

        return g




from contextlib import contextmanager


class Ctx:
    def __init__(self, nc, block, sems):
        self.nc = nc
        self.block = block
        self.sems = sems


@contextmanager
def _phase_ctx(nc, cx):
    if cx is None:
        with (
            nc.semaphore("s_d") as _sd,
            nc.semaphore("s_t") as _st,
            nc.semaphore("s_v") as _sv,
            nc.semaphore("s_s") as _ss,
            nc.semaphore("s_g") as _sg,
            nc.semaphore("s_o") as _so,
            nc.Block() as block,
        ):
            yield block, (Sem(_sd), Sem(_st), Sem(_sv), Sem(_ss), Sem(_sg), Sem(_so))
    else:
        yield cx.block, cx.sems


def emit_fence(block, sems):
    for engname in ("sync", "tensor", "vector", "scalar", "gpsimd"):
        def _f(e, sems=sems):
            for s in sems:
                s.wait(e)
        getattr(block, engname)(_f)



def build_l1(cx=None, io=None):
    if cx is None:
        nc = bass.Bass("TRN2", target_bir_lowering=False)
        io = {}
        io["x_f"] = nc.dram_tensor("x_f", [128, DJ, TOK + 3], F32, kind="ExternalInput").ap()
        io["w_in"] = nc.dram_tensor("w_in", [128, DJ, 32, 128], F32, kind="ExternalInput").ap()
        io["w_delta"] = nc.dram_tensor("w_delta", [128, IJ, IJ, 128], F32, kind="ExternalInput").ap()
        io["w_bc"] = nc.dram_tensor("w_bc", [128, IJ, 32], F32, kind="ExternalInput").ap()
        io["ln_g"] = nc.dram_tensor("ln_g", [128, DJ], F32, kind="ExternalInput").ap()
        io["ln_b"] = nc.dram_tensor("ln_b", [128, DJ], F32, kind="ExternalInput").ap()
        io["conv_w"] = nc.dram_tensor("conv_w", [128, IJ, 4], F32, kind="ExternalInput").ap()
        io["conv_b"] = nc.dram_tensor("conv_b", [128, IJ], F32, kind="ExternalInput").ap()
        io["b_delta"] = nc.dram_tensor("b_delta", [128, IJ], F32, kind="ExternalInput").ap()
        io["ones_col"] = nc.dram_tensor("ones_col", [128, 1], F32, kind="ExternalInput").ap()
        io["ones_row"] = nc.dram_tensor("ones_row", [1, 128], F32, kind="ExternalInput").ap()
        io["halo_mask"] = nc.dram_tensor("halo_mask", [128, 1], F32, kind="ExternalInput").ap()
        io["o_delta"] = nc.dram_tensor("o_delta", [128, IJ, TOK], F32, kind="ExternalOutput").ap()
        io["o_dx"] = nc.dram_tensor("o_dx", [128, IJ, TOK], F32, kind="ExternalOutput").ap()
        io["o_xb"] = nc.dram_tensor("o_xb", [128, IJ, TOK], F32, kind="ExternalOutput").ap()
        io["o_z"] = nc.dram_tensor("o_z", [128, IJ, TOK], F32, kind="ExternalOutput").ap()
        io["o_bc"] = nc.dram_tensor("o_bc", [32, TOK], F32, kind="ExternalOutput").ap()
        io["xbb_hbm"] = nc.dram_tensor("xbb_hbm", [128, IJ, TOK], BF16, kind="Internal").ap()
        io["odelta_slice"] = lambda m, ci: io["o_delta"][:, m, ci * CH:(ci + 1) * CH]
        io["odx_slice"] = lambda m, ci: io["o_dx"][:, m, ci * CH:(ci + 1) * CH]
    else:
        nc = cx.nc
    x_f = io["x_f"]; w_in = io["w_in"]; w_delta = io["w_delta"]; w_bc = io["w_bc"]
    ln_g = io["ln_g"]; ln_b = io["ln_b"]; conv_w = io["conv_w"]; conv_b = io["conv_b"]
    b_delta = io["b_delta"]; ones_col = io["ones_col"]; ones_row = io["ones_row"]
    halo_mask = io["halo_mask"]
    o_xb = io["o_xb"]; o_z = io["o_z"]
    o_bc = io["o_bc"]; xbb_hbm = io["xbb_hbm"]

    es = ExitStack()
    sbt = lambda name, shape, dt=F32: es.enter_context(nc.sbuf_tensor("p1" + name, shape, dt))
    psb = lambda name, shape: es.enter_context(nc.psum_tensor("p1" + name, shape, F32))

    wa = sbt("wa", [128, DJ, 32, 128], BF16)
    wstage = sbt("wstage", [128, 16, 128], F32)
    wbc = sbt("wbc", [128, IJ, 32], BF16)
    lg = sbt("lg", [128, DJ]); lb = sbt("lb", [128, DJ])
    cw = sbt("cw", [128, IJ, 4]); cb = sbt("cb", [128, IJ])
    bd = sbt("bd", [128, IJ])
    onec = sbt("onec", [128, 1]); oner = sbt("oner", [1, 128])
    hmask = sbt("hmask", [128, 1])
    kc = sbt("kc", [128, 4])
    rows = sbt("rows", [1, 8, CH])
    tail = sbt("tail", [128, IJ, 3])
    spa = sbt("spa", [128, 2, CH]); spb = sbt("spb", [128, 2, CH])
    bcst = sbt("bcst", [32, CH], BF16)
    xbf = sbt("xbf", [128, IJ, CH])
    xbb = sbt("xbb", [128, IJ, CH], BF16)
    xh = sbt("xh", [128, DJ, 3])

    ps_mu = psb("ps_mu", [128, CH]); ps_m2 = psb("ps_m2", [128, CH])
    ps_b0 = psb("ps_b0", [128, CH]); ps_b1 = psb("ps_b1", [128, CH])
    ps_mm0 = psb("ps_mm0", [128, CH]); ps_mm1 = psb("ps_mm1", [128, CH])
    ps_bc = psb("ps_bc", [32, CH])

    with _phase_ctx(nc, cx) as (block, sems):
        sd, st, sv, ss, sg, so = sems

        @block.vector
        def _(v):
            V = EC(v, sv)
            V.memset(kc[:, 0:1], EPS)
            V.memset(kc[:, 1:2], 1.0)
            V.memset(kc[:, 2:3], -1.0)
            V.memset(kc[:, 3:4], 0.5)

        @block.sync
        def _(sy):
            for t, src in [(lg, ln_g), (lb, ln_b), (cw, conv_w), (cb, conv_b),
                           (bd, b_delta), (onec, ones_col), (oner, ones_row),
                           (hmask, halo_mask)]:
                sd.inc(sy.dma_start(out=t[:], in_=src), 16)
            sd.inc(sy.dma_start(out=xh[:], in_=x_f[:, :, 0:3]), 16)

        wi_done = [sv.n, sv.n]
        for k in range(DJ):
            for hh in range(4):
                slot = (k * 4 + hh) % 2

                @block.sync
                def _(sy, k=k, hh=hh, slot=slot, m=wi_done[slot]):
                    sv.wait(sy, m)
                    sd.inc(sy.dma_start(
                        out=wstage[:, slot * 8:slot * 8 + 8, :],
                        in_=w_in[:, k, hh * 8:(hh + 1) * 8, :]), 16)

                dm = sd.n

                @block.vector
                def _(v, k=k, hh=hh, slot=slot, dm=dm):
                    V = EC(v, sv)
                    sd.wait(v, dm)
                    V.tensor_copy(out=wa[:, k, hh * 8:(hh + 1) * 8, :],
                                  in_=wstage[:, slot * 8:slot * 8 + 8, :])

                wi_done[slot] = sv.n

        @block.sync
        def _(sy):
            sv.wait(sy)
            sd.inc(sy.dma_start(out=wstage[:, 0:IJ, 0:32], in_=w_bc[:]), 16)

        @block.vector
        def _(v):
            V = EC(v, sv)
            sd.wait(v)
            V.tensor_copy(out=wbc[:], in_=wstage[:, 0:IJ, 0:32])

        def emit_ln12(xw, w, sqbuf, xn1, xn2b):
            for j in range(DJ):
                @block.scalar
                def _(s, j=j):
                    S_ = EC(s, ss)
                    if j == 0:
                        sd.wait(s)
                        so.wait(s)
                    st.wait(s)
                    S_.activation(out=sqbuf[:, 0:w], in_=xw[:, j, 0:w],
                                  func=AF.Square)

                @block.tensor
                def _(t, j=j):
                    ss.wait(t)
                    if j == 0:
                        sv.wait(t)
                    t.matmul(ps_mu[0:1, 0:w], onec[:], xw[:, j, 0:w],
                             start=(j == 0), stop=(j == DJ - 1))
                    st.inc(t.matmul(ps_m2[0:1, 0:w], onec[:], sqbuf[:, 0:w],
                                    start=(j == 0), stop=(j == DJ - 1)))

            @block.vector
            def _(v):
                V = EC(v, sv)
                st.wait(v)
                V.tensor_copy(out=rows[:, 0, 0:w], in_=ps_mu[0:1, 0:w])
                V.tensor_copy(out=rows[:, 1, 0:w], in_=ps_m2[0:1, 0:w])
                V.tensor_tensor(out=rows[:, 2, 0:w], in0=rows[:, 0, 0:w],
                                in1=rows[:, 0, 0:w], op=ALU.mult)
                V.tensor_tensor(out=rows[:, 2, 0:w], in0=rows[:, 1, 0:w],
                                in1=rows[:, 2, 0:w], op=ALU.subtract)

            @block.scalar
            def _(s):
                S_ = EC(s, ss)
                sv.wait(s)
                S_.activation(out=rows[:, 3, 0:w], in_=rows[:, 2, 0:w],
                              func=AF.Sqrt, bias=kc[0:1, 0:1])

            @block.vector
            def _(v):
                V = EC(v, sv)
                ss.wait(v)
                V.reciprocal(out=rows[:, 4, 0:w], in_=rows[:, 3, 0:w])
                V.tensor_tensor(out=rows[:, 5, 0:w], in0=rows[:, 4, 0:w],
                                in1=rows[:, 4, 0:w], op=ALU.mult)
                V.tensor_tensor(out=rows[:, 5, 0:w], in0=rows[:, 2, 0:w],
                                in1=rows[:, 5, 0:w], op=ALU.mult)

            @block.scalar
            def _(s):
                S_ = EC(s, ss)
                sv.wait(s)
                S_.activation(out=rows[:, 6, 0:w], in_=rows[:, 5, 0:w],
                              func=AF.Sqrt, bias=kc[0:1, 0:1])

            @block.vector
            def _(v):
                V = EC(v, sv)
                ss.wait(v)
                V.reciprocal(out=rows[:, 6, 0:w], in_=rows[:, 6, 0:w])
                V.tensor_tensor(out=rows[:, 6, 0:w], in0=rows[:, 4, 0:w],
                                in1=rows[:, 6, 0:w], op=ALU.mult)
                V.tensor_tensor(out=rows[:, 7, 0:w], in0=rows[:, 0, 0:w],
                                in1=rows[:, 6, 0:w], op=ALU.mult)

            @block.tensor
            def _(t):
                sv.wait(t)
                t.matmul(ps_b1[:, 0:w], oner[:], rows[:, 6, 0:w], start=True, stop=True)
                st.inc(t.matmul(ps_b0[:, 0:w], oner[:], rows[:, 7, 0:w],
                                start=True, stop=True))

            @block.vector
            def _(v):
                V = EC(v, sv)
                st.wait(v)
                for j in range(DJ):
                    V.tensor_tensor(out=xn1[:, j, 0:w], in0=xw[:, j, 0:w],
                                    in1=ps_b1[:, 0:w], op=ALU.mult)
                    V.tensor_tensor(out=xn1[:, j, 0:w], in0=xn1[:, j, 0:w],
                                    in1=ps_b0[:, 0:w], op=ALU.subtract)
                    V.tensor_scalar(out=xn2b[:, j, 0:w], in0=xn1[:, j, 0:w],
                                    scalar1=lg[:, j:j + 1], scalar2=lb[:, j:j + 1],
                                    op0=ALU.mult, op1=ALU.add)

        with (
            nc.sbuf_tensor("xt", [128, DJ, CH], F32) as xt,
            nc.sbuf_tensor("xn1_t", [128, DJ, CH], F32) as xn1,
            nc.sbuf_tensor("xn2b_t", [128, DJ, CH], BF16) as xn2b,
            nc.sbuf_tensor("xbr", [128, 2, CH + 3], F32) as xbr,
            nc.sbuf_tensor("xcv", [128, 2, CH], F32) as xcv,
            nc.sbuf_tensor("zst", [128, 2, CH], F32) as zst,
        ):
            emit_ln12(xh, 3, spa[:, 0, :], xn1, xn2b)
            for m in range(IJ):
                ps = ps_mm0 if m % 2 == 0 else ps_mm1

                @block.tensor
                def _(t, m=m, ps=ps):
                    ss.wait(t)
                    sv.wait(t)
                    mm = None
                    for k in range(DJ):
                        mm = t.matmul(ps[:, 0:3], wa[:, k, m], xn2b[:, k, 0:3],
                                      start=(k == 0), stop=(k == DJ - 1))
                    st.inc(mm)

                @block.vector
                def _(v, m=m, ps=ps):
                    V = EC(v, sv)
                    st.wait(v)
                    V.tensor_copy(out=tail[:, m, :], in_=ps[:, 0:3])

            @block.vector
            def _(v):
                V = EC(v, sv)
                V.tensor_scalar(out=tail[:].rearrange("p a b -> p (a b)"),
                                in0=tail[:].rearrange("p a b -> p (a b)"),
                                scalar1=hmask[:], scalar2=None, op0=ALU.mult)

            for ci in range(NCH):
                c0 = ci * CH

                @block.sync
                def _(sy, c0=c0):
                    sv.wait(sy)
                    ss.wait(sy)
                    st.wait(sy)
                    so.wait(sy)
                    sd.inc(sy.dma_start(out=xt[:],
                                        in_=x_f[:, :, 3 + c0:3 + c0 + CH]), 16)

                emit_ln12(xt, CH, xbf[:, 0, :], xn1, xn2b)

                xn2b_ready = sv.n
                ps_freeA = [sv.n, sv.n]
                xcv_free = [ss.n, ss.n]
                for m in range(IJ):
                    ps = ps_mm0 if m % 2 == 0 else ps_mm1
                    pp = m % 2

                    @block.tensor
                    def _(t, m=m, ps=ps, mv=ps_freeA[m % 2], ms=xn2b_ready):
                        sv.wait(t, max(mv, ms))
                        mm = None
                        for k in range(DJ):
                            mm = t.matmul(ps[:, :], wa[:, k, m], xn2b[:, k, :],
                                          start=(k == 0), stop=(k == DJ - 1))
                        st.inc(mm)

                    @block.vector
                    def _(v, m=m, ps=ps, pp=pp, mx=xcv_free[m % 2]):
                        V = EC(v, sv)
                        st.wait(v)
                        ss.wait(v, mx)
                        V.tensor_copy(out=xbr[:, pp, 0:3], in_=tail[:, m, :])
                        V.tensor_copy(out=xbr[:, pp, 3:3 + CH], in_=ps[:, :])
                        V.tensor_copy(out=tail[:, m, :], in_=xbr[:, pp, CH:CH + 3])
                        V.tensor_scalar(out=xcv[:, pp, :], in0=xbr[:, pp, 0:CH],
                                        scalar1=cw[:, m, 0:1], scalar2=None,
                                        op0=ALU.mult)
                        for kk in range(1, 4):
                            V.scalar_tensor_tensor(
                                out=xcv[:, pp, :], in0=xbr[:, pp, kk:kk + CH],
                                scalar=cw[:, m, kk:kk + 1], in1=xcv[:, pp, :],
                                op0=ALU.mult, op1=ALU.add)

                    ps_freeA[pp] = sv.n

                    @block.scalar
                    def _(s, m=m, pp=pp):
                        S_ = EC(s, ss)
                        sv.wait(s)
                        S_.activation(out=xbf[:, m, :], in_=xcv[:, pp, :],
                                      func=AF.Silu, bias=cb[:, m:m + 1])

                    xcv_free[pp] = ss.n

                    @block.vector
                    def _(v, m=m):
                        V = EC(v, sv)
                        ss.wait(v)
                        V.tensor_copy(out=xbb[:, m, :], in_=xbf[:, m, :])

                ps_freeZ = [ss.n, ss.n]
                z_free = [so.n, so.n]
                xbr_done = sv.n
                for m in range(IJ, 32):
                    ps = ps_mm0 if m % 2 == 0 else ps_mm1
                    pp = m % 2

                    @block.tensor
                    def _(t, m=m, ps=ps, ms=ps_freeZ[m % 2], mv=xbr_done):
                        ss.wait(t, ms)
                        if m < IJ + 2:
                            sv.wait(t, mv)
                        mm = None
                        for k in range(DJ):
                            mm = t.matmul(ps[:, :], wa[:, k, m], xn2b[:, k, :],
                                          start=(k == 0), stop=(k == DJ - 1))
                        st.inc(mm)

                    @block.scalar
                    def _(s, ps=ps, pp=pp, mo=z_free[m % 2]):
                        S_ = EC(s, ss)
                        st.wait(s)
                        so.wait(s, mo)
                        S_.copy(out=zst[:, pp, :], in_=ps[:, :])

                    ps_freeZ[pp] = ss.n

                    @block.sync
                    def _(sy, m=m, ci=ci, pp=pp):
                        ss.wait(sy)
                        so.inc(sy.dma_start(out=o_z[:, m - IJ, ci * CH:(ci + 1) * CH],
                                            in_=zst[:, pp, :]), 16)

                    z_free[pp] = so.n

                @block.sync
                def _(sy, ci=ci):
                    ss.wait(sy)
                    sv.wait(sy)
                    so.inc(sy.dma_start(out=o_xb[:, :, ci * CH:(ci + 1) * CH],
                                        in_=xbf[:]), 16)
                    so.inc(sy.dma_start(out=xbb_hbm[:, :, ci * CH:(ci + 1) * CH],
                                        in_=xbb[:]), 16)

        wdv = wa[:].rearrange("p a b c -> p (a b c)").rearrange(
            "p (a b c) -> p a b c", a=IJ, b=IJ)

        with (
            nc.sbuf_tensor("dstg", [128, 2, CH], F32) as dstg,
            nc.sbuf_tensor("dxstg", [128, 2, CH], F32) as dxstg,
        ):
            cast_done = [sv.n, sv.n]
            for k in range(IJ):
                for hh in range(2):
                    i = k * 2 + hh
                    slot = i % 2

                    @block.sync
                    def _(sy, k=k, hh=hh, slot=slot, m=cast_done[slot]):
                        sv.wait(sy, m)
                        sd.inc(sy.dma_start(
                            out=wstage[:, slot * 8:slot * 8 + 8, :],
                            in_=w_delta[:, k, hh * 8:(hh + 1) * 8, :]), 16)

                    dm = sd.n

                    @block.vector
                    def _(v, k=k, hh=hh, slot=slot, i=i, dm=dm):
                        V = EC(v, sv)
                        sd.wait(v, dm)
                        if i == 0:
                            st.wait(v)
                        V.tensor_copy(out=wdv[:, k, hh * 8:(hh + 1) * 8, :],
                                      in_=wstage[:, slot * 8:slot * 8 + 8, :])

                    cast_done[slot] = sv.n

            for ci in range(NCH):
                @block.sync
                def _(sy, ci=ci):
                    sv.wait(sy)
                    so.wait(sy)
                    sd.inc(sy.dma_start(out=xbb[:],
                                        in_=xbb_hbm[:, :, ci * CH:(ci + 1) * CH]), 16)
                    sd.inc(sy.dma_start(out=xbf[:],
                                        in_=o_xb[:, :, ci * CH:(ci + 1) * CH]), 16)

                @block.tensor
                def _(t, ci=ci):
                    sd.wait(t)
                    sv.wait(t)
                    ss.wait(t)
                    mm = None
                    for k in range(IJ):
                        mm = t.matmul(ps_bc[:, :], wbc[:, k], xbb[:, k, :],
                                      start=(k == 0), stop=(k == IJ - 1))
                    st.inc(mm)

                @block.scalar
                def _(s):
                    S_ = EC(s, ss)
                    st.wait(s)
                    S_.copy(out=bcst[:], in_=ps_bc[:, :])

                @block.sync
                def _(sy, ci=ci):
                    ss.wait(sy)
                    so.inc(sy.dma_start(out=o_bc[:, ci * CH:(ci + 1) * CH],
                                        in_=bcst[:]), 16)

                in_ready = max(sv.n, 0)
                ps_freeB = [ss.n, ss.n]
                sp_free = [sv.n, sv.n]
                d_free = [so.n, so.n]
                for m in range(IJ):
                    ps = ps_mm0 if m % 2 == 0 else ps_mm1
                    pp = m % 2

                    @block.tensor
                    def _(t, m=m, ps=ps, ms=ps_freeB[m % 2], mv=in_ready):
                        ss.wait(t, ms)
                        sv.wait(t, mv)
                        mm = None
                        for k in range(IJ):
                            mm = t.matmul(ps[:, :], wdv[:, k, m], xbb[:, k, :],
                                          start=(k == 0), stop=(k == IJ - 1))
                        st.inc(mm)

                    @block.scalar
                    def _(s, m=m, ps=ps, mv=sp_free[m % 2]):
                        S_ = EC(s, ss)
                        st.wait(s)
                        sv.wait(s, mv)
                        pp2 = m % 2
                        S_.activation(out=spa[:, pp2, :], in_=ps[:, :], func=AF.Abs,
                                      bias=bd[:, m:m + 1])
                        S_.activation(out=spa[:, pp2, :], in_=spa[:, pp2, :],
                                      func=AF.Exp, scale=kc[:, 2:3])
                        S_.activation(out=spa[:, pp2, :], in_=spa[:, pp2, :],
                                      func=AF.Ln, bias=kc[:, 1:2])
                        S_.activation(out=spb[:, pp2, :], in_=ps[:, :], func=AF.Relu,
                                      bias=bd[:, m:m + 1])

                    ps_freeB[pp] = ss.n

                    @block.vector
                    def _(v, m=m, pp=pp, mo=d_free[m % 2]):
                        V = EC(v, sv)
                        ss.wait(v)
                        sd.wait(v)
                        so.wait(v, mo)
                        V.tensor_tensor(out=dstg[:, pp, :], in0=spa[:, pp, :],
                                        in1=spb[:, pp, :], op=ALU.add)
                        V.tensor_tensor(out=dxstg[:, pp, :], in0=dstg[:, pp, :],
                                        in1=xbf[:, m, :], op=ALU.mult)

                    sp_free[pp] = sv.n

                    @block.sync
                    def _(sy, m=m, ci=ci, pp=pp):
                        sv.wait(sy)
                        so.inc(sy.dma_start(out=io["odelta_slice"](m, ci),
                                            in_=dstg[:, pp, :]), 16)
                        so.inc(sy.dma_start(out=io["odx_slice"](m, ci),
                                            in_=dxstg[:, pp, :]), 16)

                    d_free[pp] = so.n

        @block.sync
        def _(sy):
            so.wait(sy)

    es.close()
    return nc


def host_l1_inputs(inputs):
    x = np.asarray(inputs["L0_attn_x"], np.float32)
    w_in = np.asarray(inputs["L0_attn_W_in"], np.float32)
    w_delta = np.asarray(inputs["L0_attn_W_delta"], np.float32)
    w_b = np.asarray(inputs["L0_attn_W_b"], np.float32)
    w_c = np.asarray(inputs["L0_attn_W_c"], np.float32)
    ln_g = np.asarray(inputs["L0_attn_ln_g"], np.float32)
    ln_b = np.asarray(inputs["L0_attn_ln_b"], np.float32)
    conv_w = np.asarray(inputs["L0_attn_conv_w"], np.float32)[:, 0, :]
    conv_b = np.asarray(inputs["L0_attn_conv_b"], np.float32)
    b_delta = np.asarray(inputs["L0_attn_b_delta"], np.float32)

    w_in_l = np.ascontiguousarray(w_in.reshape(DJ, 128, 32, 128).transpose(1, 0, 2, 3))
    w_delta_l = np.ascontiguousarray(
        w_delta.reshape(IJ, 128, IJ, 128).transpose(1, 0, 2, 3))
    w_bc_l = np.ascontiguousarray(
        np.concatenate([w_b, w_c], axis=1).reshape(IJ, 128, 32).transpose(1, 0, 2))
    lg = np.ascontiguousarray(ln_g.reshape(DJ, 128).T)
    lbm = np.ascontiguousarray(ln_b.reshape(DJ, 128).T)
    cwl = np.ascontiguousarray(conv_w.reshape(IJ, 128, 4).transpose(1, 0, 2))
    cbl = np.ascontiguousarray(conv_b.reshape(IJ, 128).T)
    bdl = np.ascontiguousarray(b_delta.reshape(IJ, 128).T)
    onec = np.full((128, 1), 1.0 / D, np.float32)
    oner = np.ones((1, 128), np.float32)

    maps = []
    for c in range(8):
        b, q = c // 4, c % 4
        xt = x[b].T
        sl = np.zeros((D, TOK + 3), np.float32)
        t0 = q * TOK
        if q == 0:
            sl[:, 3:] = xt[:, 0:TOK]
            hm = np.zeros((128, 1), np.float32)
        else:
            sl[:, :] = xt[:, t0 - 3:t0 + TOK]
            hm = np.ones((128, 1), np.float32)
        x_f = np.ascontiguousarray(sl.reshape(DJ, 128, TOK + 3).transpose(1, 0, 2))
        maps.append(dict(x_f=x_f, w_in=w_in_l, w_delta=w_delta_l, w_bc=w_bc_l,
                         ln_g=lg, ln_b=lbm, conv_w=cwl, conv_b=cbl, b_delta=bdl,
                         ones_col=onec, ones_row=oner, halo_mask=hm))
    return maps


def kernel_unfused(**inputs):
    cores = list(range(8))
    r1 = run_bass_kernel_spmd(build_l1(), host_l1_inputs(inputs), core_ids=cores)
    r2 = run_bass_kernel_spmd(build_l2(), host_l2_inputs(inputs, r1.results),
                              core_ids=cores)
    r3 = run_bass_kernel_spmd(build_l3(),
                              host_l3_inputs(inputs, r1.results, r2.results),
                              core_ids=cores)
    r4 = run_bass_kernel_spmd(build_l4(), host_l4_inputs(inputs, r3.results),
                              core_ids=cores)
    out = np.zeros((B, L, D), np.float32)
    for c in range(8):
        b, q = c // 4, c % 4
        o = r4.results[c]["o_out"]
        out[b, q * TOK:(q + 1) * TOK, :] = (
            o.transpose(1, 0, 2).reshape(D, TOK).T)
    return out


if __name__ == "__main__":
    pass



TC2 = 2048
NC2 = L // TC2


def build_l2(cx=None, io=None):
    if cx is None:
        nc = bass.Bass("TRN2", target_bir_lowering=False)
        delta_rs = nc.dram_tensor("delta_rs", [2, 2, 128, L], F32, kind="ExternalInput").ap()
        dx_rs = nc.dram_tensor("dx_rs", [2, 2, 128, L], F32, kind="ExternalInput").ap()
        bc_all = nc.dram_tensor("bc_all", [2, 32, L], F32, kind="ExternalInput").ap()
        y_rs = nc.dram_tensor("y_rs", [2, 2, 128, L], F32, kind="ExternalOutput").ap()
        io = {}
        io["log_a"] = nc.dram_tensor("log_a", [2, 128, S], F32, kind="ExternalInput").ap()
        io["sel_in"] = nc.dram_tensor("sel_in", [S, S, 128], F32, kind="ExternalInput").ap()
        io["ones_row2"] = nc.dram_tensor("ones_row", [1, 128], F32, kind="ExternalInput").ap()
        HT = TC2 // 2
        io["delta_slice"] = lambda b, db, t0, hh: delta_rs[b, db, :, t0 + hh * HT:t0 + (hh + 1) * HT]
        io["dx_slice"] = lambda b, db, t0, hh: dx_rs[b, db, :, t0 + hh * HT:t0 + (hh + 1) * HT]
        io["bcB_slice"] = lambda b, t0: bc_all[b, 0:S, t0:t0 + TC2]
        io["bcC_slice"] = lambda b, t0: bc_all[b, S:32, t0:t0 + TC2]
        io["y_slice"] = lambda b, db, t0, hh: y_rs[b, db, :, t0 + hh * HT:t0 + (hh + 1) * HT]
    else:
        nc = cx.nc
    log_a = io["log_a"]
    sel_in = io["sel_in"]
    ones_row = io["ones_row2"]

    es = ExitStack()
    sbt = lambda name, shape, dt=F32: es.enter_context(nc.sbuf_tensor("p2" + name, shape, dt))

    dlt = sbt("dlt", [128, TC2])
    dxt = sbt("dxt", [128, TC2])
    ab = sbt("ab", [128, 2, TC2])
    bx = sbt("bx", [128, TC2])
    ht = sbt("ht", [128, TC2])
    hc = sbt("hc", [128, 2, TC2])
    yt = sbt("yt", [128, 2, TC2])
    bccB = sbt("bccB", [S, TC2])
    bccC = sbt("bccC", [S, TC2])
    sel = sbt("sel", [S, S, 128])
    oner = sbt("oner", [1, 128])
    lat = sbt("lat", [128, 2, S])
    acol = sbt("acol", [128, 2, S])
    hcol = sbt("hcol", [128, 2, 2, S])

    ps_B = es.enter_context(nc.psum_tensor("p2ps_B", [128, TC2], F32))
    ps_C = es.enter_context(nc.psum_tensor("p2ps_C", [128, TC2], F32))

    with _phase_ctx(nc, cx) as (block, sems):
        sd, st, sv, ss, sg, so = sems

        @block.sync
        def _(sy):
            for db in range(2):
                sd.inc(sy.dma_start(out=lat[:, db, :], in_=log_a[db]), 16)
            sd.inc(sy.dma_start(out=oner[:], in_=ones_row), 16)
            sd.inc(sy.dma_start(out=sel[:], in_=sel_in), 16)

        @block.scalar
        def _(s):
            S_ = EC(s, ss)
            sd.wait(s)
            S_.activation(out=acol[:], in_=lat[:], func=AF.Exp)

        @block.vector
        def _(v):
            V = EC(v, sv)
            ss.wait(v)
            V.tensor_scalar(out=acol[:], in0=acol[:], scalar1=-1.0,
                            scalar2=None, op0=ALU.mult)

        for b in range(2):
            for c4 in range(NC2):
                t0 = c4 * TC2

                @block.sync
                def _(sy, b=b, t0=t0):
                    sv.wait(sy)
                    st.wait(sy)
                    sd.inc(sy.dma_start(out=bccB[:], in_=io["bcB_slice"](b, t0)), 16)
                    sd.inc(sy.dma_start(out=bccC[:], in_=io["bcC_slice"](b, t0)), 16)

                for db in range(2):
                    @block.sync
                    def _(sy, b=b, db=db, t0=t0):
                        sv.wait(sy)
                        ss.wait(sy)
                        for hh in range(2):
                            hsl = slice(hh * (TC2 // 2), (hh + 1) * (TC2 // 2))
                            sd.inc(sy.dma_start(out=dlt[:, hsl],
                                                in_=io["delta_slice"](b, db, t0, hh)), 16)
                            sd.inc(sy.dma_start(out=dxt[:, hsl],
                                                in_=io["dx_slice"](b, db, t0, hh)), 16)

                    for si in range(S):
                        pp = si % 2

                        @block.tensor
                        def _(t, b=b, si=si):
                            sd.wait(t)
                            sv.wait(t)
                            mm = None
                            for p in range(TC2 // CH):
                                t.matmul(ps_B[:, p * CH:(p + 1) * CH], sel[:, si, :],
                                         bccB[:, p * CH:(p + 1) * CH],
                                         start=True, stop=True)
                                mm = t.matmul(ps_C[:, p * CH:(p + 1) * CH],
                                              sel[:, si, :],
                                              bccC[:, p * CH:(p + 1) * CH],
                                              start=True, stop=True)
                            st.inc(mm)

                        @block.scalar
                        def _(s, db=db, si=si, pp=pp):
                            S_ = EC(s, ss)
                            sd.wait(s)
                            sv.wait(s)
                            S_.activation(out=ab[:, pp, :], in_=dlt[:], func=AF.Exp,
                                          scale=acol[:, db, si:si + 1])

                        @block.vector
                        def _(v, b=b, db=db, c4=c4, si=si, pp=pp):
                            V = EC(v, sv)
                            st.wait(v)
                            ss.wait(v)
                            sg.wait(v)
                            V.tensor_tensor(out=bx[:], in0=dxt[:], in1=ps_B[:, :],
                                            op=ALU.mult)
                            if c4 == 0:
                                V.tensor_tensor_scan(out=ht[:], data0=ab[:, pp, :],
                                                     data1=bx[:], initial=0.0,
                                                     op0=ALU.mult, op1=ALU.add)
                            else:
                                V.tensor_tensor_scan(out=ht[:], data0=ab[:, pp, :],
                                                     data1=bx[:],
                                                     initial=hcol[:, b, db, si:si + 1],
                                                     op0=ALU.mult, op1=ALU.add)
                            V.tensor_copy(out=hcol[:, b, db, si:si + 1],
                                          in_=ht[:, TC2 - 1:TC2])
                            V.tensor_tensor(out=hc[:, pp, :], in0=ht[:],
                                            in1=ps_C[:, :], op=ALU.mult)

                        @block.gpsimd
                        def _(g, si=si, pp=pp):
                            G = EC(g, sg)
                            sv.wait(g)
                            if si == 0:
                                so.wait(g)
                                G.tensor_copy(out=yt[:, 0, :], in_=hc[:, pp, :])
                            else:
                                G.tensor_tensor(out=yt[:, si % 2, :],
                                                in0=yt[:, (si + 1) % 2, :],
                                                in1=hc[:, pp, :], op=ALU.add)

                    @block.sync
                    def _(sy, b=b, db=db, t0=t0):
                        sg.wait(sy)
                        for hh in range(2):
                            hsl = slice(hh * (TC2 // 2), (hh + 1) * (TC2 // 2))
                            so.inc(sy.dma_start(out=io["y_slice"](b, db, t0, hh),
                                                in_=yt[:, (S - 1) % 2, hsl]), 16)

        @block.sync
        def _(sy):
            so.wait(sy)

    es.close()
    return nc


def host_l2_inputs(inputs, r1_results):
    log_A = np.asarray(inputs["L0_attn_log_A"], np.float32)
    delta_full = np.zeros((2, DI, L), np.float32)
    dx_full = np.zeros((2, DI, L), np.float32)
    bc_all = np.zeros((2, 32, L), np.float32)
    for c in range(8):
        b, q = c // 4, c % 4
        dd = r1_results[c]["o_delta"]
        dx = r1_results[c]["o_dx"]
        delta_full[b, :, q * TOK:(q + 1) * TOK] = dd.transpose(1, 0, 2).reshape(DI, TOK)
        dx_full[b, :, q * TOK:(q + 1) * TOK] = dx.transpose(1, 0, 2).reshape(DI, TOK)
        bc_all[b, :, q * TOK:(q + 1) * TOK] = r1_results[c]["o_bc"]
    oner = np.ones((1, 128), np.float32)
    sel = np.zeros((S, S, 128), np.float32)
    for si in range(S):
        sel[si, si, :] = 1.0
    maps = []
    for c in range(8):
        r0 = c * 256
        maps.append(dict(
            delta_rs=np.ascontiguousarray(
                delta_full[:, r0:r0 + 256, :].reshape(2, 2, 128, L)),
            dx_rs=np.ascontiguousarray(
                dx_full[:, r0:r0 + 256, :].reshape(2, 2, 128, L)),
            log_a=np.ascontiguousarray(log_A[r0:r0 + 256, :].reshape(2, 128, S)),
            bc_all=bc_all, sel_in=sel, ones_row=oner))
    return maps



def build_l3(cx=None, io=None):
    if cx is None:
        nc = bass.Bass("TRN2", target_bir_lowering=False)
        y_t = nc.dram_tensor("y_t", [128, IJ, TOK], F32, kind="ExternalInput").ap()
        io = {}
        io["x3"] = nc.dram_tensor("x_f", [128, DJ, TOK], F32, kind="ExternalInput").ap()
        io["xb3"] = nc.dram_tensor("xb_t", [128, IJ, TOK], F32, kind="ExternalInput").ap()
        io["z3"] = nc.dram_tensor("z_t", [128, IJ, TOK], F32, kind="ExternalInput").ap()
        io["dp"] = nc.dram_tensor("dp", [128, IJ], F32, kind="ExternalInput").ap()
        io["w_out"] = nc.dram_tensor("w_out", [128, IJ, DJ, 128], F32, kind="ExternalInput").ap()
        io["ones_col3"] = nc.dram_tensor("ones_col", [128, 1], F32, kind="ExternalInput").ap()
        io["ones_row3"] = nc.dram_tensor("ones_row", [1, 128], F32, kind="ExternalInput").ap()
        io["o_res1"] = nc.dram_tensor("o_res1", [128, DJ, TOK], F32, kind="ExternalOutput").ap()
        io["y3_slice"] = lambda m, c0: y_t[:, m, c0:c0 + CH]
    else:
        nc = cx.nc
    x_f = io["x3"]; xb_t = io["xb3"]; z_t = io["z3"]; dp = io["dp"]
    w_out = io["w_out"]; ones_col = io["ones_col3"]; ones_row = io["ones_row3"]
    o_res1 = io["o_res1"]

    es = ExitStack()
    sbt = lambda name, shape, dt=F32: es.enter_context(nc.sbuf_tensor("p3" + name, shape, dt))
    psb = lambda name, shape: es.enter_context(nc.psum_tensor("p3" + name, shape, F32))

    wq = sbt("wq", [128, IJ, DJ, 128], BF16)
    wstage = sbt("wstage", [128, DJ, 128], F32)
    dpc = sbt("dpc", [128, IJ])
    onec = sbt("onec", [128, 1]); oner = sbt("oner", [1, 128])
    kc = sbt("kc", [128, 4])
    rows = sbt("rows", [1, 4, CH])
    xt = sbt("xt", [128, DJ, CH])
    xn1 = sbt("xn1", [128, DJ, CH])
    sq = sbt("sq", [128, CH])
    ymini = sbt("ymini", [128, 2, CH])
    xbmini = sbt("xbmini", [128, 2, CH])
    zmini = sbt("zmini", [128, 2, CH])
    gz = sbt("gz", [128, 2, CH])
    ut = sbt("ut", [128, 2, CH])
    ugb = sbt("ugb", [128, IJ, CH], BF16)
    rstg = sbt("rstg", [128, DJ, CH])

    ps_mu = psb("ps_mu", [1, CH]); ps_m2 = psb("ps_m2", [1, CH])
    ps_b0 = psb("ps_b0", [128, CH]); ps_b1 = psb("ps_b1", [128, CH])
    ps_mm0 = psb("ps_mm0", [128, CH]); ps_mm1 = psb("ps_mm1", [128, CH])

    with _phase_ctx(nc, cx) as (block, sems):
        sd, st, sv, ss, sg, so = sems

        @block.vector
        def _(v):
            V = EC(v, sv)
            V.memset(kc[:, 0:1], EPS)
            V.memset(kc[:, 1:2], 1.0)

        @block.sync
        def _(sy):
            sd.inc(sy.dma_start(out=dpc[:], in_=dp), 16)
            sd.inc(sy.dma_start(out=onec[:], in_=ones_col), 16)
            sd.inc(sy.dma_start(out=oner[:], in_=ones_row), 16)

        for k in range(IJ):
            @block.sync
            def _(sy, k=k):
                sv.wait(sy)
                sd.inc(sy.dma_start(out=wstage[:], in_=w_out[:, k]), 16)

            @block.vector
            def _(v, k=k):
                V = EC(v, sv)
                sd.wait(v)
                V.tensor_copy(out=wq[:, k], in_=wstage[:])

        for ci in range(NCH):
            c0 = ci * CH

            @block.sync
            def _(sy, c0=c0):
                sv.wait(sy)
                ss.wait(sy)
                st.wait(sy)
                so.wait(sy)
                sd.inc(sy.dma_start(out=xt[:], in_=x_f[:, :, c0:c0 + CH]), 16)

            for j in range(DJ):
                @block.scalar
                def _(s, j=j):
                    S_ = EC(s, ss)
                    if j == 0:
                        sd.wait(s)
                    st.wait(s)
                    S_.activation(out=sq[:], in_=xt[:, j, :], func=AF.Square)

                @block.tensor
                def _(t, j=j):
                    ss.wait(t)
                    if j == 0:
                        sv.wait(t)
                    t.matmul(ps_mu[:, :], onec[:], xt[:, j, :],
                             start=(j == 0), stop=(j == DJ - 1))
                    st.inc(t.matmul(ps_m2[:, :], onec[:], sq[:],
                                    start=(j == 0), stop=(j == DJ - 1)))

            @block.vector
            def _(v):
                V = EC(v, sv)
                st.wait(v)
                V.tensor_copy(out=rows[:, 0, :], in_=ps_mu[:, :])
                V.tensor_copy(out=rows[:, 1, :], in_=ps_m2[:, :])
                V.tensor_tensor(out=rows[:, 2, :], in0=rows[:, 0, :],
                                in1=rows[:, 0, :], op=ALU.mult)
                V.tensor_tensor(out=rows[:, 2, :], in0=rows[:, 1, :],
                                in1=rows[:, 2, :], op=ALU.subtract)

            @block.scalar
            def _(s):
                S_ = EC(s, ss)
                sv.wait(s)
                S_.activation(out=rows[:, 3, :], in_=rows[:, 2, :],
                              func=AF.Sqrt, bias=kc[0:1, 0:1])

            @block.vector
            def _(v):
                V = EC(v, sv)
                ss.wait(v)
                V.reciprocal(out=rows[:, 2, :], in_=rows[:, 3, :])
                V.tensor_tensor(out=rows[:, 3, :], in0=rows[:, 0, :],
                                in1=rows[:, 2, :], op=ALU.mult)

            @block.tensor
            def _(t):
                sv.wait(t)
                t.matmul(ps_b1[:, :], oner[:], rows[:, 2, :], start=True, stop=True)
                st.inc(t.matmul(ps_b0[:, :], oner[:], rows[:, 3, :],
                                start=True, stop=True))

            @block.vector
            def _(v):
                V = EC(v, sv)
                st.wait(v)
                for j in range(DJ):
                    V.tensor_tensor(out=xn1[:, j, :], in0=xt[:, j, :],
                                    in1=ps_b1[:, :], op=ALU.mult)
                    V.tensor_tensor(out=xn1[:, j, :], in0=xn1[:, j, :],
                                    in1=ps_b0[:, :], op=ALU.subtract)
                    V.tensor_tensor(out=xn1[:, j, :], in0=xn1[:, j, :],
                                    in1=xt[:, j, :], op=ALU.add)

            mini_free = [sv.n, sv.n]
            gz_free = [sv.n, sv.n]
            for m in range(IJ):
                pp = m % 2

                @block.sync
                def _(sy, m=m, ci=ci, pp=pp, c0=c0, mv=mini_free[m % 2]):
                    sv.wait(sy, mv)
                    sd.inc(sy.dma_start(out=ymini[:, pp, :],
                                        in_=io["y3_slice"](m, c0)), 16)
                    sd.inc(sy.dma_start(out=xbmini[:, pp, :],
                                        in_=xb_t[:, m, c0:c0 + CH]), 16)
                    sd.inc(sy.dma_start(out=zmini[:, pp, :],
                                        in_=z_t[:, m, c0:c0 + CH]), 16)

                @block.scalar
                def _(s, m=m, pp=pp, mv=gz_free[m % 2]):
                    S_ = EC(s, ss)
                    sd.wait(s)
                    sv.wait(s, mv)
                    S_.activation(out=gz[:, pp, :], in_=zmini[:, pp, :], func=AF.Silu)

                @block.vector
                def _(v, m=m, pp=pp):
                    V = EC(v, sv)
                    sd.wait(v)
                    ss.wait(v)
                    V.scalar_tensor_tensor(out=ut[:, pp, :], in0=xbmini[:, pp, :],
                                           scalar=dpc[:, m:m + 1],
                                           in1=ymini[:, pp, :],
                                           op0=ALU.mult, op1=ALU.add)
                    V.tensor_tensor(out=ugb[:, m, :], in0=ut[:, pp, :],
                                    in1=gz[:, pp, :], op=ALU.mult)

                mini_free[pp] = sv.n
                gz_free[pp] = sv.n

            ug_ready = sv.n
            ps_free3 = [sv.n, sv.n]
            for m in range(DJ):
                ps = ps_mm0 if m % 2 == 0 else ps_mm1

                @block.tensor
                def _(t, m=m, ps=ps, mv=ps_free3[m % 2], mr=ug_ready):
                    sv.wait(t, max(mv, mr))
                    mm = None
                    for k in range(IJ):
                        mm = t.matmul(ps[:, :], wq[:, k, m], ugb[:, k, :],
                                      start=(k == 0), stop=(k == IJ - 1))
                    st.inc(mm)

                @block.vector
                def _(v, m=m, ps=ps):
                    V = EC(v, sv)
                    st.wait(v)
                    V.tensor_tensor(out=rstg[:, m, :], in0=xn1[:, m, :],
                                    in1=ps[:, :], op=ALU.add)

                ps_free3[m % 2] = sv.n

            @block.sync
            def _(sy, ci=ci):
                sv.wait(sy)
                so.inc(sy.dma_start(out=o_res1[:, :, ci * CH:(ci + 1) * CH],
                                    in_=rstg[:]), 16)

        @block.sync
        def _(sy):
            so.wait(sy)

    es.close()
    return nc


def host_l3_inputs(inputs, r1_results, r2_results):
    x = np.asarray(inputs["L0_attn_x"], np.float32)
    dp = np.asarray(inputs["L0_attn_Dp"], np.float32)
    w_out = np.asarray(inputs["L0_attn_W_out"], np.float32)

    y_full = np.zeros((2, DI, L), np.float32)
    for c in range(8):
        y_full[:, c * 256:(c + 1) * 256, :] = r2_results[c]["y_rs"].reshape(2, 256, L)

    w_out_l = np.ascontiguousarray(w_out.reshape(IJ, 128, DJ, 128).transpose(1, 0, 2, 3))
    dpl = np.ascontiguousarray(dp.reshape(IJ, 128).T)
    onec = np.full((128, 1), 1.0 / D, np.float32)
    oner = np.ones((1, 128), np.float32)

    maps = []
    for c in range(8):
        b, q = c // 4, c % 4
        xt = x[b].T[:, q * TOK:(q + 1) * TOK]
        x_f = np.ascontiguousarray(xt.reshape(DJ, 128, TOK).transpose(1, 0, 2))
        y_t = np.ascontiguousarray(
            y_full[b, :, q * TOK:(q + 1) * TOK].reshape(IJ, 128, TOK).transpose(1, 0, 2))
        maps.append(dict(x_f=x_f, y_t=y_t, xb_t=r1_results[c]["o_xb"],
                         z_t=r1_results[c]["o_z"], dp=dpl, w_out=w_out_l,
                         ones_col=onec, ones_row=oner))
    return maps



CH2 = 256
NCH2 = TOK // CH2


def build_l4(cx=None, io=None):
    if cx is None:
        nc = bass.Bass("TRN2", target_bir_lowering=False)
        io = {}
        io["res1_in"] = nc.dram_tensor("res1_in", [128, DJ, TOK], F32, kind="ExternalInput").ap()
        io["w1t"] = nc.dram_tensor("w1t", [128, DJ, HJ, 128], F32, kind="ExternalInput").ap()
        io["b1"] = nc.dram_tensor("b1", [128, HJ], F32, kind="ExternalInput").ap()
        io["w2t"] = nc.dram_tensor("w2t", [128, HJ, DJ, 128], F32, kind="ExternalInput").ap()
        io["b2"] = nc.dram_tensor("b2", [128, DJ], F32, kind="ExternalInput").ap()
        io["ones_col4"] = nc.dram_tensor("ones_col", [128, 1], F32, kind="ExternalInput").ap()
        io["ones_row4"] = nc.dram_tensor("ones_row", [1, 128], F32, kind="ExternalInput").ap()
        io["iden_in"] = nc.dram_tensor("iden_in", [128, 128], F32, kind="ExternalInput").ap()
        io["o_out"] = nc.dram_tensor("o_out", [128, DJ, TOK], F32, kind="ExternalOutput").ap()
    else:
        nc = cx.nc
    res1_in = io["res1_in"]; w1t = io["w1t"]; b1 = io["b1"]; w2t = io["w2t"]
    b2 = io["b2"]; ones_col = io["ones_col4"]; ones_row = io["ones_row4"]
    iden_in = io["iden_in"]; o_out = io["o_out"]

    es = ExitStack()
    sbt = lambda name, shape, dt=F32: es.enter_context(nc.sbuf_tensor("p4" + name, shape, dt))
    psb = lambda name, shape: es.enter_context(nc.psum_tensor("p4" + name, shape, F32))

    w1q = sbt("w1q", [128, DJ, HJ, 128], BF16)
    w2q = sbt("w2q", [128, HJ, DJ, 128], BF16)
    wstage = sbt("wstage", [128, 2, 1024], F32)
    b1c = sbt("b1c", [128, HJ]); b2c = sbt("b2c", [128, DJ])
    onec = sbt("onec", [128, 1]); oner = sbt("oner", [1, 128])
    iden = sbt("iden", [128, 128])
    kc = sbt("kc", [128, 4])
    wac = sbt("wac", [128, HJ])
    wrow = sbt("wrow", [1, 8])
    wcol = sbt("wcol", [128, 4])
    rows = sbt("rows", [1, 8, CH2])
    resc = sbt("resc", [128, DJ, CH2])
    xn3 = sbt("xn3", [128, DJ, CH2])
    sq2 = sbt("sq2", [128, CH2])
    cmx = sbt("cmx", [128, CH2])
    xst = sbt("xst", [128, 2])
    xqb = sbt("xqb", [128, DJ, CH2], BF16)
    q8 = sbt("q8", [128, CH2], I8)
    q8w = sbt("q8w", [128, 1024], I8)
    qf = sbt("qf", [128, 2, CH2])
    qfall = sbt("qfall", [128, 8, CH2])
    q8a = sbt("q8a", [128, 8 * CH2], I8)
    h1b = sbt("h1b", [128, HJ, CH2], BF16)
    dq_sb = sbt("dq_sb", [128, CH2])
    s_tok = sbt("s_tok", [128, 2])
    dq_tok = sbt("dq_tok", [128, 2])
    srow4 = sbt("srow4", [1, 512])
    dq2_sb = sbt("dq2_sb", [128, CH2])
    ostg = sbt("ostg", [128, 2, CH2])

    ps_mu = psb("ps_mu", [1, CH2]); ps_m2 = psb("ps_m2", [1, CH2])
    ps_b0 = psb("ps_b0", [128, CH2]); ps_b1 = psb("ps_b1", [128, CH2])
    ps_mm0 = psb("ps_mm0", [128, CH2]); ps_mm1 = psb("ps_mm1", [128, CH2])
    ps_tr = psb("ps_tr", [128, 512])
    ps_sc = psb("ps_sc", [128, 2])

    with _phase_ctx(nc, cx) as (block, sems):
        sd, st, sv, ss, sg, so = sems

        @block.vector
        def _(v):
            V = EC(v, sv)
            V.memset(kc[:, 0:1], EPS)
            V.memset(kc[:, 1:2], 1.0)
            V.memset(kc[:, 2:3], -1.49)
            V.memset(kc[:, 3:4], 1.49)


        @block.sync
        def _(sy):
            for t, src in [(b1c, b1), (b2c, b2), (onec, ones_col), (oner, ones_row),
                           (iden, iden_in)]:
                sd.inc(sy.dma_start(out=t[:], in_=src), 16)

        def emit_wprep(w_src, wq_dst, ws_i, wsr_i, nelem):
            wsf = w_src.rearrange("p a b c -> p (a b c)")
            wqf = wq_dst[:].rearrange("p a b c -> p (a b c)")
            total = wsf.shape[1]
            PW = 1024
            npiece = total // PW
            w_free = [sv.n, sv.n]
            for k in range(npiece):
                wp = k % 2

                @block.sync
                def _(sy, k=k, wp=wp, m=w_free[k % 2]):
                    sv.wait(sy, m)
                    ss.wait(sy)
                    sd.inc(sy.dma_start(out=wstage[:, wp, :],
                                        in_=wsf[:, k * PW:(k + 1) * PW]), 16)

                @block.vector
                def _(v, k=k, wp=wp):
                    V = EC(v, sv)
                    sd.wait(v)
                    V.tensor_reduce(out=wac[:, k:k + 1], in_=wstage[:, wp, :],
                                    axis=AX.X, op=ALU.add, apply_absolute_value=True)

                w_free[wp] = sv.n

            @block.vector
            def _(v, npiece=npiece):
                V = EC(v, sv)
                V.tensor_reduce(out=wac[:, 0:1], in_=wac[:, 0:npiece],
                                axis=AX.X, op=ALU.add)

            @block.tensor
            def _(t):
                sv.wait(t)
                st.inc(t.matmul(ps_sc[0:1, 0:1], kc[:, 1:2], wac[:, 0:1],
                                start=True, stop=True))

            @block.vector
            def _(v, ws_i=ws_i, wsr_i=wsr_i, nelem=nelem):
                V = EC(v, sv)
                st.wait(v)
                V.tensor_scalar(out=wrow[:, ws_i:ws_i + 1], in0=ps_sc[0:1, 0:1],
                                scalar1=1.0 / nelem, scalar2=EPS,
                                op0=ALU.mult, op1=ALU.max)
                V.reciprocal(out=wrow[:, wsr_i:wsr_i + 1],
                             in_=wrow[:, ws_i:ws_i + 1])

            @block.tensor
            def _(t, ws_i=ws_i, wsr_i=wsr_i):
                sv.wait(t)
                t.matmul(ps_sc[:, 0:1], oner[:], wrow[:, ws_i:ws_i + 1],
                         start=True, stop=True)
                st.inc(t.matmul(ps_sc[:, 1:2], oner[:], wrow[:, wsr_i:wsr_i + 1],
                                start=True, stop=True))

            @block.vector
            def _(v, ws_i=ws_i):
                V = EC(v, sv)
                st.wait(v)
                V.tensor_copy(out=wcol[:, 0:1], in_=ps_sc[:, 1:2])
                V.tensor_copy(out=wcol[:, 1 + ws_i // 2:2 + ws_i // 2],
                              in_=ps_sc[:, 0:1])

            w_free = [sv.n, sv.n]
            s_done = [ss.n, ss.n]
            for k in range(npiece):
                wp = k % 2

                @block.sync
                def _(sy, k=k, wp=wp, m=w_free[k % 2], m2=s_done[k % 2]):
                    sv.wait(sy, m)
                    ss.wait(sy, m2)
                    sd.inc(sy.dma_start(out=wstage[:, wp, :],
                                        in_=wsf[:, k * PW:(k + 1) * PW]), 16)

                @block.vector
                def _(v, k=k, wp=wp):
                    V = EC(v, sv)
                    sd.wait(v)
                    V.tensor_scalar(out=wstage[:, wp, :], in0=wstage[:, wp, :],
                                    scalar1=wcol[:, 0:1], scalar2=None,
                                    op0=ALU.mult)
                    V.tensor_scalar(out=wstage[:, wp, :], in0=wstage[:, wp, :],
                                    scalar1=kc[:, 2:3], scalar2=kc[:, 3:4],
                                    op0=ALU.max, op1=ALU.min)

                w_free[wp] = sv.n

                @block.scalar
                def _(s, k=k, wp=wp):
                    S_ = EC(s, ss)
                    sv.wait(s)
                    S_.copy(out=q8w[:], in_=wstage[:, wp, :])
                    S_.copy(out=wqf[:, k * PW:(k + 1) * PW], in_=q8w[:])

                s_done[wp] = ss.n

        emit_wprep(w1t, w1q, 0, 1, float(H * D))
        emit_wprep(w2t, w2q, 2, 3, float(H * D))

        resc_free = max(sv.n, 1)
        for ci in range(NCH2):
            c0 = ci * CH2

            @block.sync
            def _(sy, c0=c0, mv=resc_free):
                sv.wait(sy, mv)
                st.wait(sy)
                sd.inc(sy.dma_start(out=resc[:], in_=res1_in[:, :, c0:c0 + CH2]), 16)

            for j in range(DJ):
                @block.scalar
                def _(s, j=j):
                    S_ = EC(s, ss)
                    if j == 0:
                        sd.wait(s)
                    st.wait(s)
                    S_.activation(out=sq2[:], in_=resc[:, j, :], func=AF.Square)

                @block.tensor
                def _(t, j=j):
                    ss.wait(t)
                    if j == 0:
                        sv.wait(t)
                    t.matmul(ps_mu[:, :], onec[:], resc[:, j, :],
                             start=(j == 0), stop=(j == DJ - 1))
                    st.inc(t.matmul(ps_m2[:, :], onec[:], sq2[:],
                                    start=(j == 0), stop=(j == DJ - 1)))

            @block.vector
            def _(v):
                V = EC(v, sv)
                st.wait(v)
                V.tensor_copy(out=rows[:, 0, :], in_=ps_mu[:, :])
                V.tensor_copy(out=rows[:, 1, :], in_=ps_m2[:, :])
                V.tensor_tensor(out=rows[:, 2, :], in0=rows[:, 0, :],
                                in1=rows[:, 0, :], op=ALU.mult)
                V.tensor_tensor(out=rows[:, 2, :], in0=rows[:, 1, :],
                                in1=rows[:, 2, :], op=ALU.subtract)

            @block.scalar
            def _(s):
                S_ = EC(s, ss)
                sv.wait(s)
                S_.activation(out=rows[:, 3, :], in_=rows[:, 2, :],
                              func=AF.Sqrt, bias=kc[0:1, 0:1])

            @block.vector
            def _(v):
                V = EC(v, sv)
                ss.wait(v)
                V.reciprocal(out=rows[:, 2, :], in_=rows[:, 3, :])
                V.tensor_tensor(out=rows[:, 3, :], in0=rows[:, 0, :],
                                in1=rows[:, 2, :], op=ALU.mult)

            @block.tensor
            def _(t):
                sv.wait(t)
                t.matmul(ps_b1[:, :], oner[:], rows[:, 2, :], start=True, stop=True)
                st.inc(t.matmul(ps_b0[:, :], oner[:], rows[:, 3, :],
                                start=True, stop=True))

            @block.vector
            def _(v):
                V = EC(v, sv)
                st.wait(v)
                for j in range(DJ):
                    V.tensor_tensor(out=xn3[:, j, :], in0=resc[:, j, :],
                                    in1=ps_b1[:, :], op=ALU.mult)
                    V.tensor_tensor(out=xn3[:, j, :], in0=xn3[:, j, :],
                                    in1=ps_b0[:, :], op=ALU.subtract)

            def emit_quant(src, nj, qdst, ws_i, dqdst):
                @block.vector
                def _(v, src=src, nj=nj):
                    V = EC(v, sv)
                    ss.wait(v)
                    V.tensor_reduce(out=cmx[:], in_=src[:].rearrange("p j t -> p t j"),
                                    axis=AX.X, op=ALU.max, apply_absolute_value=True)

                @block.tensor
                def _(t):
                    sv.wait(t)
                    st.wait(t)
                    t.transpose(ps_tr[:, 0:128], cmx[:, 0:128], iden[:])
                    st.inc(t.transpose(ps_tr[:, 128:256], cmx[:, 128:256], iden[:]))

                @block.vector
                def _(v):
                    V = EC(v, sv)
                    st.wait(v)
                    V.tensor_reduce(out=xst[:, 0:1], in_=ps_tr[:, 0:128],
                                    axis=AX.X, op=ALU.max)
                    V.tensor_reduce(out=xst[:, 1:2], in_=ps_tr[:, 128:256],
                                    axis=AX.X, op=ALU.max)

                @block.vector
                def _(v, ws_i=ws_i):
                    V = EC(v, sv)
                    V.tensor_scalar(out=xst[:], in0=xst[:], scalar1=EPS,
                                    scalar2=None, op0=ALU.max)
                    V.reciprocal(out=s_tok[:], in_=xst[:])
                    V.tensor_scalar(out=s_tok[:], in0=s_tok[:],
                                    scalar1=127.0, scalar2=None, op0=ALU.mult)
                    V.tensor_scalar(out=dq_tok[:], in0=xst[:],
                                    scalar1=wcol[:, 1 + ws_i // 2:2 + ws_i // 2],
                                    scalar2=1.0 / 127.0, op0=ALU.mult, op1=ALU.mult)

                @block.tensor
                def _(t):
                    sv.wait(t)
                    t.transpose(ps_tr[0:1, 0:128], s_tok[:, 0:1], iden[:])
                    t.transpose(ps_tr[0:1, 128:256], s_tok[:, 1:2], iden[:])
                    t.transpose(ps_tr[0:1, 256:384], dq_tok[:, 0:1], iden[:])
                    st.inc(t.transpose(ps_tr[0:1, 384:512], dq_tok[:, 1:2], iden[:]))

                @block.vector
                def _(v):
                    V = EC(v, sv)
                    st.wait(v)
                    V.tensor_copy(out=srow4[:], in_=ps_tr[0:1, :])

                @block.tensor
                def _(t):
                    sv.wait(t)
                    t.matmul(ps_b0[:, 0:128], oner[:], srow4[:, 0:128],
                             start=True, stop=True)
                    t.matmul(ps_b0[:, 128:256], oner[:], srow4[:, 128:256],
                             start=True, stop=True)
                    t.matmul(ps_b1[:, 0:128], oner[:], srow4[:, 256:384],
                             start=True, stop=True)
                    st.inc(t.matmul(ps_b1[:, 128:256], oner[:],
                                    srow4[:, 384:512], start=True, stop=True))

                @block.vector
                def _(v, dqdst=dqdst):
                    V = EC(v, sv)
                    st.wait(v)
                    V.tensor_copy(out=dqdst[:], in_=ps_b1[:, :])

                qa_free = ss.n
                for j0 in range(0, nj, 8):
                    @block.vector
                    def _(v, j0=j0, src=src, m=qa_free):
                        V = EC(v, sv)
                        ss.wait(v, m)
                        for j in range(j0, j0 + 8):
                            V.tensor_tensor(out=qfall[:, j - j0, :], in0=src[:, j, :],
                                            in1=ps_b0[:, :], op=ALU.mult)

                    @block.scalar
                    def _(s, j0=j0, qdst=qdst):
                        S_ = EC(s, ss)
                        sv.wait(s)
                        S_.copy(out=q8a[:], in_=qfall[:].rearrange("p a b -> p (a b)"))
                        S_.copy(out=qdst[:, j0:j0 + 8, :].rearrange("p a b -> p (a b)"),
                                in_=q8a[:])

                    qa_free = ss.n

            emit_quant(xn3, DJ, xqb, 0, dq_sb)

            xqb_ready = ss.n
            ps_free2 = [sv.n, sv.n]
            qa_free = ss.n
            for m in range(HJ):
                ps = ps_mm0 if m % 2 == 0 else ps_mm1

                @block.tensor
                def _(t, m=m, ps=ps, mv=ps_free2[m % 2], ms=xqb_ready):
                    ss.wait(t, ms)
                    sv.wait(t, mv)
                    mm = None
                    for k in range(DJ):
                        mm = t.matmul(ps[:, :], w1q[:, k, m], xqb[:, k, :],
                                      start=(k == 0), stop=(k == DJ - 1))
                    st.inc(mm)

                @block.vector
                def _(v, m=m, ps=ps, mq=qa_free):
                    V = EC(v, sv)
                    st.wait(v)
                    ss.wait(v, mq)
                    V.tensor_tensor(out=qfall[:, m % 8, :], in0=ps[:, :],
                                    in1=dq_sb[:], op=ALU.mult)
                    V.tensor_scalar(out=qfall[:, m % 8, :], in0=qfall[:, m % 8, :],
                                    scalar1=b1c[:, m:m + 1], scalar2=None,
                                    op0=ALU.add)

                ps_free2[m % 2] = sv.n

                if m % 8 == 7:
                    @block.scalar
                    def _(s, m0=m - 7):
                        S_ = EC(s, ss)
                        sv.wait(s)
                        S_.activation(
                            out=h1b[:, m0:m0 + 8, :].rearrange("p a b -> p (a b)"),
                            in_=qfall[:].rearrange("p a b -> p (a b)"),
                            func=AF.Gelu)

                    qa_free = ss.n

            emit_quant(h1b, HJ, h1b, 2, dq2_sb)

            hq_ready = ss.n
            ps_free2 = [sv.n, sv.n]
            o_free = [so.n, so.n]
            for m in range(DJ):
                ps = ps_mm0 if m % 2 == 0 else ps_mm1

                @block.tensor
                def _(t, m=m, ps=ps, mv=ps_free2[m % 2], ms=hq_ready):
                    ss.wait(t, ms)
                    sv.wait(t, mv)
                    mm = None
                    for k in range(HJ):
                        mm = t.matmul(ps[:, :], w2q[:, k, m], h1b[:, k, :],
                                      start=(k == 0), stop=(k == HJ - 1))
                    st.inc(mm)

                @block.vector
                def _(v, m=m, ps=ps, mo=o_free[m % 2]):
                    V = EC(v, sv)
                    st.wait(v)
                    so.wait(v, mo)
                    V.tensor_tensor(out=qf[:, m % 2, :], in0=ps[:, :], in1=dq2_sb[:],
                                    op=ALU.mult)
                    V.scalar_tensor_tensor(out=ostg[:, m % 2, :], in0=qf[:, m % 2, :],
                                           scalar=b2c[:, m:m + 1],
                                           in1=resc[:, m, :],
                                           op0=ALU.add, op1=ALU.add)

                ps_free2[m % 2] = sv.n

                @block.sync
                def _(sy, m=m, ci=ci):
                    sv.wait(sy)
                    so.inc(sy.dma_start(
                        out=o_out[:, m, ci * CH2:(ci + 1) * CH2],
                        in_=ostg[:, m % 2, :]), 16)

                o_free[m % 2] = so.n

            resc_free = sv.n

        @block.sync
        def _(sy):
            so.wait(sy)

    es.close()
    return nc


def host_l4_inputs(inputs, r3_results):
    w1 = np.asarray(inputs["L0_ffn_W1"], np.float32)
    b1 = np.asarray(inputs["L0_ffn_b1"], np.float32)
    w2 = np.asarray(inputs["L0_ffn_W2"], np.float32)
    b2 = np.asarray(inputs["L0_ffn_b2"], np.float32)

    w1t = np.ascontiguousarray(w1.T.reshape(DJ, 128, HJ, 128).transpose(1, 0, 2, 3))
    w2t = np.ascontiguousarray(w2.T.reshape(HJ, 128, DJ, 128).transpose(1, 0, 2, 3))
    b1l = np.ascontiguousarray(b1.reshape(HJ, 128).T)
    b2l = np.ascontiguousarray(b2.reshape(DJ, 128).T)
    onec = np.full((128, 1), 1.0 / D, np.float32)
    oner = np.ones((1, 128), np.float32)
    iden = np.eye(128, dtype=np.float32)

    maps = []
    for c in range(8):
        maps.append(dict(res1_in=r3_results[c]["o_res1"], w1t=w1t, b1=b1l,
                         w2t=w2t, b2=b2l, ones_col=onec, ones_row=oner,
                         iden_in=iden))
    return maps



def build_fused():
    nc = bass.Bass("TRN2", target_bir_lowering=False)
    ein = lambda n, s, dt=F32: nc.dram_tensor(n, s, dt, kind="ExternalInput").ap()
    itn = lambda n, s, dt=F32: nc.dram_tensor(n, s, dt, kind="Internal").ap()

    x_f = ein("x_f", [128, DJ, TOK + 3])
    w_in = ein("w_in", [128, DJ, 32, 128])
    w_delta = ein("w_delta", [128, IJ, IJ, 128])
    w_bc = ein("w_bc", [128, IJ, 32])
    ln_g = ein("ln_g", [128, DJ]); ln_b = ein("ln_b", [128, DJ])
    conv_w = ein("conv_w", [128, IJ, 4]); conv_b = ein("conv_b", [128, IJ])
    b_delta = ein("b_delta", [128, IJ])
    ones_col = ein("ones_col", [128, 1]); ones_row = ein("ones_row", [1, 128])
    halo_mask = ein("halo_mask", [128, 1])
    log_a = ein("log_a", [128, IJ, S])
    m1 = ein("m1", [128, 64])
    m2 = ein("m2", [128, 8])
    sel_in = ein("sel_in", [S, S, 128], BF16)
    dp = ein("dp", [128, IJ])
    w_out = ein("w_out", [128, IJ, DJ, 128])
    w1t = ein("w1t", [128, DJ, HJ, 128]); b1 = ein("b1", [128, HJ])
    w2t = ein("w2t", [128, HJ, DJ, 128]); b2 = ein("b2", [128, DJ])
    iden_in = ein("iden_in", [128, 128])
    o_out = nc.dram_tensor("o_out", [128, DJ, TOK], F32, kind="ExternalOutput").ap()

    d_hbm = itn("d_hbm", [128, IJ, TOK])
    dx_hbm = itn("dx_hbm", [128, IJ, TOK])
    xb_hbm = itn("xb_hbm", [128, IJ, TOK])
    z_hbm = itn("z_hbm", [128, IJ, TOK])
    bc_hbm = itn("bc_hbm", [32, TOK], BF16)
    xbb_hbm = itn("xbb_hbm", [128, IJ, TOK], BF16)
    y0_hbm = itn("y0_hbm", [128, IJ, TOK])
    y_full = itn("y_full", [128, IJ, TOK])
    states_send = itn("states_send", [128, IJ, 2 * S])
    states_gath = itn("states_gath", [8, 128, IJ, 2 * S])
    res1_hbm = itn("res1_hbm", [128, DJ, TOK])

    with (
        nc.semaphore("s_d") as _sd,
        nc.semaphore("s_t") as _st,
        nc.semaphore("s_v") as _sv,
        nc.semaphore("s_s") as _ss,
        nc.semaphore("s_g") as _sg,
        nc.semaphore("s_o") as _so,
        nc.semaphore("s_cc") as _cc,
        nc.Block() as block,
    ):
        sems = (Sem(_sd), Sem(_st), Sem(_sv), Sem(_ss), Sem(_sg), Sem(_so))
        sd, st, sv, ss, sg, so = sems
        cx = Ctx(nc, block, sems)

        io1 = dict(x_f=x_f, w_in=w_in, w_delta=w_delta, w_bc=w_bc, ln_g=ln_g,
                   ln_b=ln_b, conv_w=conv_w, conv_b=conv_b, b_delta=b_delta,
                   ones_col=ones_col, ones_row=ones_row, halo_mask=halo_mask,
                   o_xb=xb_hbm, o_z=z_hbm, o_bc=bc_hbm, xbb_hbm=xbb_hbm,
                   odelta_slice=lambda m, ci: d_hbm[:, m, ci * CH:(ci + 1) * CH],
                   odx_slice=lambda m, ci: dx_hbm[:, m, ci * CH:(ci + 1) * CH])
        build_l1(cx, io1)

        emit_fence(block, sems)

        io2 = dict(d_hbm=d_hbm, dx_hbm=dx_hbm, bc_hbm=bc_hbm, log_a=log_a,
                   sel_in=sel_in, m1=m1, m2=m2, y0_hbm=y0_hbm, y_full=y_full,
                   states_send=states_send, states_gath=states_gath, cc_sem=_cc)
        emit_p2(cx, io2)

        emit_fence(block, sems)

        io3 = dict(
            x3=x_f[:, :, 3:3 + TOK], xb3=xb_hbm, z3=z_hbm, dp=dp, w_out=w_out,
            ones_col3=ones_col, ones_row3=ones_row, o_res1=res1_hbm,
            y3_slice=lambda m, c0: y_full[:, m, c0:c0 + CH])
        build_l3(cx, io3)

        emit_fence(block, sems)

        io4 = dict(res1_in=res1_hbm, w1t=w1t, b1=b1, w2t=w2t, b2=b2,
                   ones_col4=ones_col, ones_row4=ones_row, iden_in=iden_in,
                   o_out=o_out)
        build_l4(cx, io4)

        @block.sync
        def _(sy):
            so.wait(sy)

    return nc


def host_fused_inputs(inputs):
    maps1 = host_l1_inputs(inputs)
    log_A = np.asarray(inputs["L0_attn_log_A"], np.float32)
    dpv = np.asarray(inputs["L0_attn_Dp"], np.float32)
    w_out = np.asarray(inputs["L0_attn_W_out"], np.float32)
    w1 = np.asarray(inputs["L0_ffn_W1"], np.float32)
    b1 = np.asarray(inputs["L0_ffn_b1"], np.float32)
    w2 = np.asarray(inputs["L0_ffn_W2"], np.float32)
    b2 = np.asarray(inputs["L0_ffn_b2"], np.float32)

    w_out_l = np.ascontiguousarray(w_out.reshape(IJ, 128, DJ, 128).transpose(1, 0, 2, 3))
    dpl = np.ascontiguousarray(dpv.reshape(IJ, 128).T)
    w1t = np.ascontiguousarray(w1.T.reshape(DJ, 128, HJ, 128).transpose(1, 0, 2, 3))
    w2t = np.ascontiguousarray(w2.T.reshape(HJ, 128, DJ, 128).transpose(1, 0, 2, 3))
    b1l = np.ascontiguousarray(b1.reshape(HJ, 128).T)
    b2l = np.ascontiguousarray(b2.reshape(DJ, 128).T)
    import ml_dtypes
    iden = np.eye(128, dtype=np.float32)
    sel = np.zeros((S, S, 128), ml_dtypes.bfloat16)
    for si in range(S):
        sel[si, si, :] = 1.0

    log_a_full = np.ascontiguousarray(log_A.reshape(IJ, 128, S).transpose(1, 0, 2))
    maps = []
    for c in range(8):
        m = dict(maps1[c])
        b, q = c // 4, c % 4
        m1m = np.zeros((8, 8), np.float32)
        m2m = np.zeros((8,), np.float32)
        for r in range(8):
            rb, rq = r // 4, r % 4
            if rb == b and rq < q:
                m2m[r] = 1.0
                for k in range(8):
                    kb, kq = k // 4, k % 4
                    if kb == b and rq < kq < q:
                        m1m[k, r] = 1.0
        m.update(log_a=log_a_full, sel_in=sel, dp=dpl, w_out=w_out_l, w1t=w1t,
                 b1=b1l, w2t=w2t, b2=b2l, iden_in=iden,
                 m1=np.ascontiguousarray(np.tile(m1m.reshape(1, 64), (128, 1))),
                 m2=np.ascontiguousarray(np.tile(m2m.reshape(1, 8), (128, 1))))
        maps.append(m)
    return maps


def kernel(**inputs):
    cores = list(range(8))
    nc = build_fused()
    r = run_bass_kernel_spmd(nc, host_fused_inputs(inputs), core_ids=cores)
    out = np.zeros((B, L, D), np.float32)
    for c in range(8):
        b, q = c // 4, c % 4
        o = r.results[c]["o_out"]
        out[b, q * TOK:(q + 1) * TOK, :] = o.transpose(1, 0, 2).reshape(D, TOK).T
    return out





GG = 4


def emit_p2(cx, io):
    nc = cx.nc
    block = cx.block
    sd, st, sv, ss, sg, so = cx.sems
    _cc = io["cc_sem"]

    es = ExitStack()
    sbt = lambda name, shape, dt=F32: es.enter_context(nc.sbuf_tensor("s2" + name, shape, dt))

    dl4 = sbt("dl4", [128, GG, TOK])
    dx4 = sbt("dx4", [128, GG, TOK])
    ab = sbt("ab", [128, 2, TOK])
    bx = sbt("bx", [128, TOK])
    h0 = sbt("h0", [128, 2, TOK])
    hc = sbt("hc", [128, 2, TOK])
    ypp = sbt("ypp", [128, GG, TOK])
    bccB = sbt("bccB", [S, TOK], BF16)
    bccC = sbt("bccC", [S, TOK], BF16)
    sel = sbt("sel", [S, S, 128], BF16)
    lat = sbt("lat", [128, IJ, S])
    acol = sbt("acol", [128, IJ, S])
    hend = sbt("hend", [128, IJ, S])
    dlog = sbt("dlog", [128, IJ, S])
    pn = sbt("pn", [128, 1])
    hin = sbt("hin", [128, IJ, S])
    m1t = sbt("m1t", [128, 64])
    m2t = sbt("m2t", [128, 8])
    sgt = sbt("sgt", [128, 8, IJ, 2 * S])
    lacc = sbt("lacc", [128, 2, IJ, S])
    hacc = sbt("hacc", [128, 2, IJ, S])
    wr = sbt("wr", [128, IJ, S])

    ps_B = es.enter_context(nc.psum_tensor("s2ps_B", [128, TOK], F32))
    ps_C = es.enter_context(nc.psum_tensor("s2ps_C", [128, TOK], F32))

    @block.sync
    def _(sy):
        so.wait(sy)
        sv.wait(sy)
        ss.wait(sy)
        st.wait(sy)
        sd.inc(sy.dma_start(out=lat[:], in_=io["log_a"]), 16)
        sd.inc(sy.dma_start(out=sel[:], in_=io["sel_in"]), 16)
        sd.inc(sy.dma_start(out=m1t[:], in_=io["m1"]), 16)
        sd.inc(sy.dma_start(out=m2t[:], in_=io["m2"]), 16)
        sd.inc(sy.dma_start(out=bccB[:], in_=io["bc_hbm"][0:S, :]), 16)
        sd.inc(sy.dma_start(out=bccC[:], in_=io["bc_hbm"][S:32, :]), 16)

    @block.scalar
    def _(s):
        S_ = EC(s, ss)
        sd.wait(s)
        S_.activation(out=acol[:], in_=lat[:], func=AF.Exp)

    @block.vector
    def _(v):
        V = EC(v, sv)
        ss.wait(v)
        V.tensor_scalar(out=acol[:], in0=acol[:], scalar1=-1.0, scalar2=None,
                        op0=ALU.mult)

    for gg in range(IJ // GG):
        @block.sync
        def _(sy, gg=gg):
            sv.wait(sy)
            ss.wait(sy)
            so.wait(sy)
            sd.inc(sy.dma_start(out=dl4[:],
                                in_=io["d_hbm"][:, gg * GG:(gg + 1) * GG, :]), 16)
            sd.inc(sy.dma_start(out=dx4[:],
                                in_=io["dx_hbm"][:, gg * GG:(gg + 1) * GG, :]), 16)

        for gl in range(GG):
            g = gg * GG + gl

            @block.vector
            def _(v, g=g, gl=gl):
                V = EC(v, sv)
                sd.wait(v)
                ss.wait(v)
                sg.wait(v)
                if gl == 0:
                    V.memset(bx[:], 1.0)
                V.tensor_tensor_scan(out=h0[:, 0, :], data0=bx[:],
                                     data1=dl4[:, gl, :], initial=0.0,
                                     op0=ALU.mult, op1=ALU.add)
                V.tensor_copy(out=pn[:], in_=h0[:, 0, TOK - 1:TOK])
                V.tensor_scalar(out=dlog[:, g, :], in0=acol[:, g, :],
                                scalar1=pn[:], scalar2=None, op0=ALU.mult)

        ab_free = [sv.n, sv.n]
        hc_free = [sg.n, sg.n]
        ps_free = sv.n
        for si in range(S):
            @block.tensor
            def _(t, si=si, pv=ps_free):
                sd.wait(t)
                sv.wait(t, pv)
                mm = None
                for p in range(TOK // CH):
                    t.matmul(ps_B[:, p * CH:(p + 1) * CH], sel[:, si, :],
                             bccB[:, p * CH:(p + 1) * CH], start=True, stop=True)
                    mm = t.matmul(ps_C[:, p * CH:(p + 1) * CH], sel[:, si, :],
                                  bccC[:, p * CH:(p + 1) * CH], start=True,
                                  stop=True)
                st.inc(mm)

            for gl in range(GG):
                g = gg * GG + gl
                hpp = (si * GG + gl) % 2

                @block.scalar
                def _(s, g=g, gl=gl, si=si, hpp=hpp, m=ab_free[(si * GG + gl) % 2]):
                    S_ = EC(s, ss)
                    sd.wait(s)
                    sv.wait(s, m)
                    S_.activation(out=ab[:, hpp, :], in_=dl4[:, gl, :], func=AF.Exp,
                                  scale=acol[:, g, si:si + 1])

                @block.vector
                def _(v, g=g, gl=gl, si=si, hpp=hpp, hm=hc_free[(si * GG + gl) % 2]):
                    V = EC(v, sv)
                    st.wait(v)
                    ss.wait(v)
                    sg.wait(v, hm)
                    V.tensor_tensor(out=bx[:], in0=dx4[:, gl, :], in1=ps_B[:, :],
                                    op=ALU.mult)
                    V.tensor_tensor_scan(out=h0[:, 0, :], data0=ab[:, hpp, :],
                                         data1=bx[:], initial=0.0,
                                         op0=ALU.mult, op1=ALU.add)
                    V.tensor_copy(out=hend[:, g, si:si + 1],
                                  in_=h0[:, 0, TOK - 1:TOK])
                    V.tensor_tensor(out=hc[:, hpp, :], in0=h0[:, 0, :],
                                    in1=ps_C[:, :], op=ALU.mult)

                ab_free[hpp] = sv.n
                if gl == GG - 1:
                    ps_free = sv.n

                @block.gpsimd
                def _(gp, si=si, gl=gl, hpp=hpp):
                    G = EC(gp, sg)
                    sv.wait(gp)
                    if si == 0:
                        so.wait(gp)
                        G.tensor_copy(out=ypp[:, gl, :], in_=hc[:, hpp, :])
                    else:
                        G.tensor_tensor(out=ypp[:, gl, :], in0=ypp[:, gl, :],
                                        in1=hc[:, hpp, :], op=ALU.add)

                hc_free[hpp] = sg.n

        @block.sync
        def _(sy, gg=gg):
            sg.wait(sy)
            sv.wait(sy)
            for gl in range(GG):
                so.inc(sy.dma_start(out=io["y0_hbm"][:, gg * GG + gl, :],
                                    in_=ypp[:, gl, :]), 16)

    @block.sync
    def _(sy):
        sv.wait(sy)
        so.inc(sy.dma_start(out=io["states_send"][:, :, 0:S], in_=dlog[:]), 16)
        so.inc(sy.dma_start(out=io["states_send"][:, :, S:2 * S], in_=hend[:]), 16)

    @block.gpsimd
    def _(gp):
        so.wait(gp)
        sg.wait(gp)
        gp.collective_compute("AllGather", ALU.bypass,
                              replica_groups=[list(range(8))],
                              ins=[io["states_send"].opt()],
                              outs=[io["states_gath"].opt()]).then_inc(_cc, 1)

    @block.sync
    def _(sy):
        sy.wait_ge(_cc, 1)
        sd.inc(sy.dma_start(out=sgt[:], in_=io["states_gath"].rearrange("r p a b -> p r a b")), 16)

    @block.vector
    def _(v):
        V = EC(v, sv)
        sd.wait(v)
        V.memset(lacc[:, 0], 0.0)
        V.memset(hacc[:, 0], 0.0)

    for r in range(8):
        @block.vector
        def _(v, r=r):
            V = EC(v, sv)
            for k in range(8):
                V.scalar_tensor_tensor(
                    out=lacc[:, (k + 1) % 2],
                    in0=sgt[:, k, :, 0:S],
                    scalar=m1t[:, k * 8 + r:k * 8 + r + 1],
                    in1=lacc[:, k % 2], op0=ALU.mult, op1=ALU.add)

        @block.scalar
        def _(s, r=r):
            S_ = EC(s, ss)
            sv.wait(s)
            S_.activation(out=wr[:], in_=lacc[:, 0], func=AF.Exp)

        @block.vector
        def _(v, r=r):
            V = EC(v, sv)
            ss.wait(v)
            V.scalar_tensor_tensor(
                out=wr[:], in0=wr[:], scalar=m2t[:, r:r + 1],
                in1=sgt[:, r, :, S:2 * S],
                op0=ALU.mult, op1=ALU.mult)
            V.tensor_tensor(out=hacc[:, (r + 1) % 2], in0=hacc[:, r % 2],
                            in1=wr[:], op=ALU.add)
            if r < 7:
                V.memset(lacc[:, 0], 0.0)

    @block.vector
    def _(v):
        V = EC(v, sv)
        V.tensor_copy(out=hin[:], in_=hacc[:, 0])

    VQ2 = 1024
    for gg in range(IJ // GG):
        @block.sync
        def _(sy, gg=gg):
            sv.wait(sy)
            ss.wait(sy)
            so.wait(sy)
            sd.inc(sy.dma_start(out=dl4[:],
                                in_=io["d_hbm"][:, gg * GG:(gg + 1) * GG, :]), 16)

        for gl in range(GG):
            @block.vector
            def _(v, gl=gl):
                V = EC(v, sv)
                sd.wait(v)
                if gl == 0:
                    V.memset(bx[:], 1.0)
                V.tensor_tensor_scan(out=dx4[:, gl, :], data0=bx[:],
                                     data1=dl4[:, gl, :], initial=0.0,
                                     op0=ALU.mult, op1=ALU.add)

        ab_free = [sv.n, sv.n]
        hc_free = [sg.n, sg.n]
        ps_free = sv.n
        for si in range(S):
            @block.tensor
            def _(t, si=si, ps_free=ps_free):
                sd.wait(t)
                sv.wait(t, ps_free)
                mm = None
                for p in range(TOK // CH):
                    mm = t.matmul(ps_C[:, p * CH:(p + 1) * CH], sel[:, si, :],
                                  bccC[:, p * CH:(p + 1) * CH], start=True,
                                  stop=True)
                st.inc(mm)

            for gl in range(GG):
                g = gg * GG + gl
                hpp = (si * GG + gl) % 2

                @block.scalar
                def _(s, g=g, gl=gl, si=si, hpp=hpp, m=ab_free[(si * GG + gl) % 2]):
                    S_ = EC(s, ss)
                    sv.wait(s, m)
                    S_.activation(out=ab[:, hpp, :], in_=dx4[:, gl, :], func=AF.Exp,
                                  scale=acol[:, g, si:si + 1])

                @block.vector
                def _(v, g=g, gl=gl, si=si, hpp=hpp):
                    V = EC(v, sv)
                    st.wait(v)
                    ss.wait(v)
                    if si == 0:
                        so.wait(v)
                        V.scalar_tensor_tensor(out=ypp[:, gl, :],
                                               in0=ab[:, hpp, :],
                                               scalar=hin[:, g, si:si + 1],
                                               in1=ps_C[:, :], op0=ALU.mult,
                                               op1=ALU.mult)
                    else:
                        V.scalar_tensor_tensor(out=hc[:, 0, :], in0=ab[:, hpp, :],
                                               scalar=hin[:, g, si:si + 1],
                                               in1=ps_C[:, :], op0=ALU.mult,
                                               op1=ALU.mult)
                        V.tensor_tensor(out=ypp[:, gl, :], in0=ypp[:, gl, :],
                                        in1=hc[:, 0, :], op=ALU.add)

                ab_free[hpp] = sv.n
                if gl == GG - 1:
                    ps_free = sv.n

        for gl in range(GG):
            g = gg * GG + gl

            @block.sync
            def _(sy, g=g):
                sg.wait(sy)
                sv.wait(sy)
                sd.inc(sy.dma_start(out=bx[:], in_=io["y0_hbm"][:, g, :]), 16)

            @block.vector
            def _(v, g=g, gl=gl):
                V = EC(v, sv)
                sd.wait(v)
                sg.wait(v)
                so.wait(v)
                V.tensor_tensor(out=h0[:, 0, :], in0=bx[:], in1=ypp[:, gl, :],
                                op=ALU.add)

            @block.sync
            def _(sy, g=g):
                sv.wait(sy)
                so.inc(sy.dma_start(out=io["y_full"][:, g, :], in_=h0[:, 0, :]), 16)

    es.close()
